# version 14
# baseline (speedup 1.0000x reference)
import numpy as np
import ml_dtypes
import concourse.bass as bass
import concourse.mybir as mybir
from concourse.bass_utils import run_bass_kernel_spmd

F32 = mybir.dt.float32
BF16 = mybir.dt.bfloat16
I32 = mybir.dt.int32
AF = mybir.ActivationFunctionType
OP = mybir.AluOpType
AX = mybir.AxisListType

D = 2048
T = 2048
NS = 256
OH = 8192
DT = D // 128
PAIRS = OH // 128
NQ = 4
EPS = 1e-5
QB = 127.0
TWO_PI = 6.283185307179586

ENGINES = ("sync", "gpsimd", "vector", "scalar", "tensor")
GR = 2048


def R(name, e0, e1, es=4):
    b0, b1 = e0 * es, e1 * es
    return [(name, i) for i in range(b0 // GR, -(-b1 // GR))]


class B:
    def __init__(self):
        self.stream = {e: [] for e in ENGINES}
        self.semnames = []
        self.cnt = {}
        self.waited = {e: {} for e in ENGINES}
        self.lastw = {}
        self.readers = {}
        self.pending = {e: [] for e in ENGINES}
        self.qrot = {}

    def sem(self, name):
        self.semnames.append(name)
        self.cnt[name] = 0

    def _wait(self, eng, sem, val):
        if val <= 0 or self.waited[eng].get(sem, 0) >= val:
            return
        self.waited[eng][sem] = val
        self.stream[eng].append(("w", sem, val))

    def _deps(self, eng, reads, writes):
        for r in reads:
            lw = self.lastw.get(r)
            if lw:
                self._wait(eng, lw[0], lw[1])
        for r in writes:
            lw = self.lastw.get(r)
            if lw:
                self._wait(eng, lw[0], lw[1])
            for rd in self.readers.get(r, ()):
                self._wait(eng, rd[0], rd[1])

    def op(self, eng, fn, reads=(), writes=(), inc=True):
        self._deps(eng, reads, writes)
        if inc:
            sem = "E" + eng
            self.cnt[sem] += 1
            v = self.cnt[sem]
            self.stream[eng].append(("o", fn, sem, 1))
            tok = (sem, v)
            for r in self.pending[eng]:
                self.readers.setdefault(r, []).append(tok)
            self.pending[eng] = []
            for r in reads:
                self.readers.setdefault(r, []).append(tok)
            for r in writes:
                self.lastw[r] = tok
                self.readers[r] = []
        else:
            self.stream[eng].append(("o", fn, None, 0))
            self.pending[eng].extend(reads)
            self.pending[eng].extend(writes)

    def dma(self, eng, fn, reads=(), writes=(), q=None):
        self._deps(eng, reads, writes)
        base = q or ("D" + eng)
        idx = self.qrot.get(base, 0)
        self.qrot[base] = (idx + 1) % 8
        sem = f"{base}.{idx}"
        self._wait(eng, sem, self.cnt[sem])
        self.cnt[sem] += 16
        v = self.cnt[sem]
        self.stream[eng].append(("d", fn, sem, 16))
        tok = (sem, v)
        for r in reads:
            self.readers.setdefault(r, []).append(tok)
        for r in writes:
            self.lastw[r] = tok
            self.readers[r] = []


def build_program(debug=False):
    nc = bass.Bass("TRN2", target_bir_lowering=False)
    b = B()
    for e in ENGINES:
        b.sem("E" + e)
    for q in ("Dsync", "Dgpsimd", "Dscalar", "Dout", "Dwq"):
        for i in range(8):
            b.sem(f"{q}.{i}")

    xT_h = nc.dram_tensor("xT", [D, T], F32, kind="ExternalInput").ap()
    BreT_h = nc.dram_tensor("BreT", [D, NS], F32, kind="ExternalInput").ap()
    BimT_h = nc.dram_tensor("BimT", [D, NS], F32, kind="ExternalInput").ap()
    CreT_h = nc.dram_tensor("CreT", [NS, D], F32, kind="ExternalInput").ap()
    CimT_h = nc.dram_tensor("CimT", [NS, D], F32, kind="ExternalInput").ap()
    nul_h = nc.dram_tensor("nul", [1, NS], F32, kind="ExternalInput").ap()
    thl_h = nc.dram_tensor("thl", [1, NS], F32, kind="ExternalInput").ap()
    Dv_h = nc.dram_tensor("Dv", [1, D], F32, kind="ExternalInput").ap()
    nw_h = nc.dram_tensor("nw", [1, D], F32, kind="ExternalInput").ap()
    wgvL_h = nc.dram_tensor("wgvL", [128, 128, DT, 128], F32,
                            kind="ExternalInput").ap()
    woutT_h = nc.dram_tensor("woutT", [OH, D], F32, kind="ExternalInput").ap()

    id_h = nc.inline_tensor(np.eye(128, dtype=np.float32), "ident").ap()
    idb_h = nc.inline_tensor(np.eye(128, dtype=ml_dtypes.bfloat16), "identb").ap()
    iota_h = nc.inline_tensor(np.arange(T, dtype=np.float32).reshape(1, T), "iota").ap()
    ones_h = nc.inline_tensor(np.ones((1, 128), np.float32), "onesr").ap()
    onesc_h = nc.inline_tensor(np.ones((128, 1), np.float32), "onescf").ap()

    ik = "ExternalOutput" if debug else "Internal"
    h2sp = nc.dram_tensor("h2sp", [OH, T], BF16, kind=ik).ap()
    x1sp = nc.dram_tensor("x1sp", [D, T], F32, kind=ik).ap()
    z2sp = nc.dram_tensor("z2sp", [D, T], F32, kind=ik).ap()
    wq2sp = nc.dram_tensor("wq2sp", [8, 8, 128, 8, 256], BF16, kind=ik).ap()
    if debug:
        tapBu = nc.dram_tensor("tapBu", [2, 128, 4096], F32, kind="ExternalOutput").ap()
        tapH = nc.dram_tensor("tapH", [2, 128, 4096], BF16, kind="ExternalOutput").ap()
        tapDQ = nc.dram_tensor("tapDQ", [128, T], F32, kind="ExternalOutput").ap()
        tapAM = nc.dram_tensor("tapAM", [1, T], F32, kind="ExternalOutput").ap()
        tapA2 = nc.dram_tensor("tapA2", [128, DT * T], mybir.dt.bfloat16,
                               kind="ExternalOutput").ap()
    outT_h = nc.dram_tensor("outT", [D, T], F32, kind="ExternalOutput").ap()

    h2sp_t = h2sp.rearrange("(j p) t -> p j t", p=128)
    x1sp_t = x1sp.rearrange("(a p) t -> p a t", p=128)
    z2sp_t = z2sp.rearrange("(a p) t -> p a t", p=128)
    out_t = outT_h.rearrange("(a p) t -> p a t", p=128)
    wout_t = woutT_h.rearrange("(j p) d -> p j d", p=128)

    sb = lambda n, s, d=F32: nc.alloc_sbuf_tensor(n, s, d)
    c_id = sb("c_id", [128, 128])
    c_idb = sb("c_idb", [128, 128], BF16)
    c_ones = sb("c_ones", [1, 128])
    c_onesc = sb("c_onesc", [128, 1])
    rows = sb("rows", [1, 1792])
    srow = sb("srow", [1, 8])
    colD = sb("colD", [128, DT])
    colNW = sb("colNW", [128, DT])
    colG = sb("colG", [128, 2])
    colM = sb("colM", [128, 2])
    colS = sb("colS", [128, 2])
    colS2 = sb("colS2", [128, 2])
    colSG = sb("colSG", [128, 1])
    slots = sb("slots", [128, 192])

    P1 = sb("P1", [128, DT * T], BF16)
    P2 = sb("P2", [128, 8192])
    P3 = sb("P3", [128, 4096])
    HBF = sb("HBF", [128, 2 * T], BF16)
    HBF2 = sb("HBF2", [128, 2 * T], BF16)
    CBF = sb("CBF", [128, 2 * D], BF16)
    CBF2 = sb("CBF2", [128, 2 * D], BF16)
    WOF = sb("WOF", [128, 2048])
    WR = sb("WR", [128, 4096], BF16)
    CH_HS = sb("CH_HS", [128, 512])
    CH_I = sb("CH_I", [128, 512], I32)
    CH_CF = sb("CH_CF", [128, 512])
    MXA = sb("MXA", [128, 2048])
    MNA = sb("MNA", [128, 2048])
    Z2S = sb("Z2S", [128, 2048])
    OTS = sb("OTS", [128, 1024])
    REP = sb("REP", [128, 1024])

    WV = P2
    BBF = P2[:, 0:4096].bitcast(BF16)
    dqa1rep = P3[:, 0:2048]
    WOQ = P3[:, 2048:4096].bitcast(BF16)
    r_amax2 = P3[0:1, 0:2048]
    SG1 = HBF[:, 0:2048]
    SG2 = HBF[:, 2048:4096]
    GS = HBF2[:, 0:2048].bitcast(F32)
    SIL = HBF2[:, 2048:4096].bitcast(F32)
    T2 = CBF[:, 0:2048].bitcast(F32)
    H2T = CBF[:, 2048:4096]
    H2Q = CBF[:, 2048:3072]
    X1S = CBF2[:, 0:2048].bitcast(F32)
    MXB = MXA[:, 0:1024].bitcast(BF16)
    MNB = MXA[:, 1024:2048].bitcast(BF16)
    SREP = MXA[:, 1536:2048]
    srep_row = MXA[0:1, 1536:2048]
    c_iota = Z2S[0:1, 0:2048]

    def WQap(k):
        if k < 2:
            return CBF2[:, k * 2048:(k + 1) * 2048]
        return MNA[:, (k - 2) * 1024:(k - 1) * 1024].bitcast(BF16)

    def WQreg(k):
        if k < 2:
            return R("CBF2", k * 2048, (k + 1) * 2048, es=2)
        return R("MNA", (k - 2) * 1024, (k - 1) * 1024, es=4)

    psA = nc.alloc_psum_tensor("psA", [128, 2048], F32)
    psB = nc.alloc_psum_tensor("psB", [128, 2048], F32)
    PSA = lambda k: psA[:, k * 512:(k + 1) * 512]
    PSB = lambda k: psB[:, k * 512:(k + 1) * 512]
    RA = lambda k: ("psA", k)
    RB = lambda k: ("psB", k)

    def bcast_row(row_ap, rreads, psk, pskreg, out_ap, outwrites, scale=None):
        b.op("tensor", lambda t: t.matmul(psk, c_ones[:, 0:128], row_ap,
                                          start=True, stop=True),
             reads=list(rreads) + [("c_ones", 0)], writes=[pskreg])
        if scale is None:
            b.op("vector", lambda v: v.tensor_copy(out_ap, psk),
                 reads=[pskreg], writes=list(outwrites))
        else:
            b.op("vector", lambda v: v.tensor_scalar(out_ap, psk, scale, None,
                                                     OP.mult),
                 reads=[pskreg], writes=list(outwrites))

    RN = [
        (CH_HS[:, 0:512], [("CH_HS", 0)],
         CH_I[:, 0:512], [("CH_I", 0)], CH_CF[:, 0:512], [("CH_CF", 0)]),
        (MNA[:, 1024:1536], R("MNA", 1024, 1536),
         MNA[:, 0:512].bitcast(I32), R("MNA", 0, 512),
         MNA[:, 512:1024], R("MNA", 512, 1024)),
    ]

    def round_neg(k, dst_ap, dst_regs):
        src_ap, src_regs, i_ap, i_regs, cf_ap, cf_regs = RN[k]
        b.op("vector", lambda v: v.tensor_copy(i_ap, src_ap),
             reads=src_regs, writes=i_regs)
        b.op("vector", lambda v: v.tensor_copy(cf_ap, i_ap),
             reads=i_regs, writes=cf_regs)
        b.op("vector", lambda v: v.tensor_tensor(src_ap, src_ap,
                                                 cf_ap, OP.subtract),
             reads=src_regs + cf_regs, writes=src_regs)
        b.op("vector", lambda v: v.scalar_tensor_tensor(
            cf_ap, src_ap, 0.5, cf_ap, OP.is_ge, OP.add),
            reads=src_regs + cf_regs, writes=cf_regs)
        b.op("vector", lambda v: v.scalar_tensor_tensor(
            dst_ap, src_ap, -0.5, cf_ap, OP.is_le, OP.subtract),
            reads=src_regs + cf_regs, writes=dst_regs)

    def rs(i):
        return rows[0:1, i * 512:(i + 1) * 512]

    def pa(i):
        return rows[0:1, i * 256:(i + 1) * 256]

    def repr_(s):
        return REP[0:1, s * 512:(s + 1) * 512]

    theta_row = rows[0:1, 1536:1792]
    RTH = [("rows", 3)]

    b.op("vector", lambda v: v.memset(srow[0:1, 4:5], EPS), writes=[("srow", 0)])

    for part in range(4):
        lo = part * 4
        b.dma("gpsimd", lambda g, lo=lo: g.dma_start(
            out=P1.ap().rearrange("p (a t) -> p a t", a=DT)[:, lo:lo + 4, :],
            in_=xT_h.rearrange("(a p) t -> p a t", p=128)[:, lo:lo + 4, :]),
            writes=R("P1", lo * T, (lo + 4) * T, es=2))

    b.dma("sync", lambda s: s.dma_start(out=c_id[:], in_=id_h[:]), writes=[("c_id", 0)])
    b.dma("sync", lambda s: s.dma_start(out=c_idb[:], in_=idb_h[:]), writes=[("c_idb", 0)])
    b.dma("sync", lambda s: s.dma_start(out=c_ones[:], in_=ones_h[:]), writes=[("c_ones", 0)])
    b.dma("sync", lambda s: s.dma_start(out=c_onesc[:], in_=onesc_h[:]), writes=[("c_onesc", 0)])
    b.dma("sync", lambda s: s.dma_start(out=pa(0), in_=nul_h[:]), writes=R("rows", 0, 256))
    b.dma("sync", lambda s: s.dma_start(out=pa(6), in_=thl_h[:]), writes=RTH)

    b.op("scalar", lambda s: s.activation(pa(1), pa(0), AF.Abs),
         reads=R("rows", 0, 256), writes=R("rows", 256, 512))
    b.op("scalar", lambda s: s.activation(pa(2), pa(1), AF.Exp, scale=-1.0),
         reads=R("rows", 256, 512), writes=R("rows", 512, 768))
    b.op("vector", lambda v: v.tensor_scalar(pa(2), pa(2), 1.0, None, OP.add),
         reads=R("rows", 512, 768), writes=R("rows", 512, 768))
    b.op("scalar", lambda s: s.activation(pa(2), pa(2), AF.Ln),
         reads=R("rows", 512, 768), writes=R("rows", 512, 768))
    b.op("vector", lambda v: v.tensor_scalar(pa(1), pa(0), 0.0, None, OP.max),
         reads=R("rows", 0, 256), writes=R("rows", 256, 512))
    b.op("vector", lambda v: v.tensor_tensor(pa(1), pa(1), pa(2), OP.add),
         reads=R("rows", 256, 768), writes=R("rows", 256, 512))
    b.op("scalar", lambda s: s.activation(pa(2), pa(1), AF.Exp),
         reads=R("rows", 256, 512), writes=R("rows", 512, 768))
    b.op("scalar", lambda s: s.activation(pa(2), pa(2), AF.Exp, scale=-1.0),
         reads=R("rows", 512, 768), writes=R("rows", 512, 768))
    b.op("scalar", lambda s: s.activation(theta_row, pa(6), AF.Exp),
         reads=RTH, writes=RTH)
    b.op("scalar", lambda s: s.activation(pa(3), pa(2), AF.Square),
         reads=R("rows", 512, 768), writes=R("rows", 768, 1024))
    b.op("vector", lambda v: v.tensor_scalar(pa(3), pa(3), -1.0, 1.0, OP.mult, OP.add),
         reads=R("rows", 768, 1024), writes=R("rows", 768, 1024))
    b.op("vector", lambda v: v.tensor_scalar(pa(3), pa(3), 0.0, None, OP.max),
         reads=R("rows", 768, 1024), writes=R("rows", 768, 1024))
    b.op("scalar", lambda s: s.activation(pa(3), pa(3), AF.Sqrt),
         reads=R("rows", 768, 1024), writes=R("rows", 768, 1024))

    for nh in range(2):
        b.op("tensor", lambda t, nh=nh: t.transpose(
            PSB(3)[:, nh:nh + 1], rows[0:1, 512 + nh * 128:512 + (nh + 1) * 128],
            c_id[0:1, 0:1]),
            reads=R("rows", 512, 768) + [("c_id", 0)], writes=[RB(3)])
        b.op("tensor", lambda t, nh=nh: t.transpose(
            PSB(3)[:, 2 + nh:3 + nh], rows[0:1, 768 + nh * 128:768 + (nh + 1) * 128],
            c_id[0:1, 0:1]),
            reads=R("rows", 768, 1024) + [("c_id", 0)], writes=[RB(3)])
    b.op("vector", lambda v: v.tensor_copy(colM[:], PSB(3)[:, 0:2]),
         reads=[RB(3)], writes=[("colM", 0)])
    b.op("vector", lambda v: v.tensor_copy(colG[:], PSB(3)[:, 2:4]),
         reads=[RB(3)], writes=[("colG", 0)])

    for which, src_h, coltgt, colreg in ((0, Dv_h, colD, ("colD", 0)),
                                         (1, nw_h, colNW, ("colNW", 0))):
        for k in range(4):
            b.dma("sync", lambda s, src_h=src_h, k=k: s.dma_start(
                out=rs(2), in_=src_h[0:1, k * 512:(k + 1) * 512]),
                writes=R("rows", 1024, 1536))
            for j in range(4):
                b.op("tensor", lambda t, k=k, j=j: t.transpose(
                    PSB(2)[:, 4 * k + j:4 * k + j + 1],
                    rows[0:1, 1024 + j * 128:1024 + (j + 1) * 128], c_id[0:1, 0:1]),
                    reads=R("rows", 1024, 1536) + [("c_id", 0)], writes=[RB(2)])
        if which == 0:
            b.op("vector", lambda v, coltgt=coltgt: v.tensor_scalar(
                coltgt[:], PSB(2)[:, 0:DT], 1.0, None, OP.add),
                reads=[RB(2)], writes=[colreg])
        else:
            b.op("vector", lambda v, coltgt=coltgt: v.tensor_copy(
                coltgt[:], PSB(2)[:, 0:DT]),
                reads=[RB(2)], writes=[colreg])

    b.dma("sync", lambda s: s.dma_start(
        out=P2[:, 4096:8192].rearrange("p (a n) -> p a n", a=DT),
        in_=BreT_h.rearrange("(a p) n -> p a n", p=128)),
        writes=R("P2", 4096, 8192))
    b.op("vector", lambda v: v.tensor_copy(BBF[:, 0:4096], P2[:, 4096:8192]),
         reads=R("P2", 4096, 8192), writes=R("P2", 0, 2048))
    b.dma("sync", lambda s: s.dma_start(
        out=P2[:, 4096:8192].rearrange("p (a n) -> p a n", a=DT),
        in_=BimT_h.rearrange("(a p) n -> p a n", p=128)),
        writes=R("P2", 4096, 8192))
    b.op("vector", lambda v: v.tensor_copy(BBF[:, 4096:8192], P2[:, 4096:8192]),
         reads=R("P2", 4096, 8192), writes=R("P2", 2048, 4096))

    for nh in range(2):
        CT = CBF if nh == 0 else CBF2
        cn = "CBF" if nh == 0 else "CBF2"
        b.dma("sync", lambda s, nh=nh: s.dma_start(
            out=Z2S[:], in_=CreT_h[nh * 128:(nh + 1) * 128, :]),
            writes=R("Z2S", 0, 2048))
        b.op("vector", lambda v, CT=CT: v.tensor_copy(CT[:, 0:2048], Z2S[:]),
             reads=R("Z2S", 0, 2048), writes=R(cn, 0, 2048, es=2))
        b.dma("sync", lambda s, nh=nh: s.dma_start(
            out=Z2S[:], in_=CimT_h[nh * 128:(nh + 1) * 128, :]),
            writes=R("Z2S", 0, 2048))
        b.op("vector", lambda v, CT=CT: v.tensor_scalar(
            CT[:, 2048:4096], Z2S[:], -1.0, None, OP.mult),
            reads=R("Z2S", 0, 2048), writes=R(cn, 2048, 4096, es=2))
    b.dma("sync", lambda s: s.dma_start(out=c_iota, in_=iota_h[:]),
          writes=R("Z2S", 0, 2048))

    wout4 = woutT_h.rearrange("(r p) f -> r p f", p=128)
    NGV = 128
    NWO = wout4.shape[0]

    def wgv_chunk(ci):
        return wgvL_h[ci].rearrange("p a o -> p (a o)")

    def wout_chunk(ci):
        return wout4[ci]

    WRF = WR[:, 0:4096].bitcast(F32)

    def emit_abs_chunk(src_ap, slot):
        if slot % 2 == 0:
            buf, breg = WOF[:], R("WOF", 0, 2048)
        else:
            buf, breg = WRF[:, 0:2048], R("WR", 0, 2048)
        b.dma("scalar", lambda s, src_ap=src_ap, buf=buf: s.dma_start(
            out=buf, in_=src_ap), writes=breg, q="Dscalar")
        b.op("scalar", lambda s, slot=slot, buf=buf: s.activation(
            buf, buf, AF.Abs, accum_out=slots[:, slot:slot + 1]),
            reads=breg, writes=breg + [("slots", slot)])

    for ci in range(24):
        emit_abs_chunk(wgv_chunk(ci), ci)

    bre_ap = P2[:, 4096:6144]
    bim_ap = P2[:, 6144:8192]
    BRE = R("P2", 4096, 6144)
    BIM = R("P2", 6144, 8192)
    za, zb = P3[:, 0:2048], P3[:, 2048:4096]
    ZA = R("P3", 0, 2048)
    ZB = R("P3", 2048, 4096)
    cos_ap, sin_ap = MXA[:], MNA[:]
    COS = R("MXA", 0, 2048)
    SIN = R("MNA", 0, 2048)

    TRS = [
        (CH_HS[:], [("CH_HS", 0)], CH_I[:], [("CH_I", 0)],
         CH_CF[:], [("CH_CF", 0)], PSB(0), RB(0)),
        (OTS[:, 0:512], [("OTS", 0)], OTS[:, 512:1024].bitcast(I32), [("OTS", 1)],
         REP[:, 0:512], [("REP", 0)], PSB(1), RB(1)),
    ]
    ti = 0
    for nh in range(2):
        for which, dst, dregs in ((0, cos_ap, COS), (1, sin_ap, SIN)):
            for q in range(NQ):
                u_ap, u_r, i_ap, i_r, cf_ap, cf_r, ps, psr = TRS[ti % 2]
                ti += 1
                b.op("tensor", lambda t, nh=nh, q=q, ps=ps: t.matmul(
                    ps, theta_row[0:1, nh * 128:(nh + 1) * 128],
                    c_iota[0:1, q * 512:(q + 1) * 512], start=True, stop=True),
                    reads=RTH + R("Z2S", 0, 2048), writes=[psr])
                off = 0.25 if which == 0 else 0.0
                b.op("vector", lambda v, off=off, u_ap=u_ap, ps=ps: v.tensor_scalar(
                    u_ap, ps, 1.0 / TWO_PI, off, OP.mult, OP.add),
                    reads=[psr], writes=u_r)
                b.op("vector", lambda v, i_ap=i_ap, u_ap=u_ap: v.tensor_copy(
                    i_ap, u_ap), reads=u_r, writes=i_r)
                b.op("gpsimd", lambda g, cf_ap=cf_ap, i_ap=i_ap: g.tensor_copy(
                    cf_ap, i_ap), reads=i_r, writes=cf_r)
                b.op("vector", lambda v, u_ap=u_ap, cf_ap=cf_ap: v.tensor_tensor(
                    u_ap, u_ap, cf_ap, OP.subtract),
                    reads=u_r + cf_r, writes=u_r)
                b.op("vector", lambda v, u_ap=u_ap, cf_ap=cf_ap: v.scalar_tensor_tensor(
                    cf_ap, u_ap, 0.5, u_ap, OP.is_gt, OP.subtract),
                    reads=u_r, writes=cf_r)
                b.op("scalar", lambda s, dst=dst, q=q, cf_ap=cf_ap: s.activation(
                    dst[:, q * 512:(q + 1) * 512], cf_ap, AF.Sin, scale=-TWO_PI),
                    reads=cf_r, writes=[dregs[q]])
        for q in range(NQ):
            for comp in range(2):
                for a in range(DT):
                    b.op("tensor", lambda t, nh=nh, q=q, comp=comp, a=a: t.matmul(
                        PSA(comp),
                        BBF[:, comp * 4096 + a * 256 + nh * 128:
                            comp * 4096 + a * 256 + (nh + 1) * 128],
                        P1[:, a * T + q * 512:a * T + (q + 1) * 512],
                        start=(a == 0), stop=(a == DT - 1)),
                        reads=R("P2", comp * 2048 + a * 128, comp * 2048 + (a + 1) * 128) +
                              R("P1", a * T + q * 512, a * T + (q + 1) * 512, es=2),
                        writes=[RA(comp)], inc=(a == DT - 1))
            for comp, dst_ap, dregs in ((0, bre_ap, BRE), (1, bim_ap, BIM)):
                b.op("vector", lambda v, q=q, comp=comp, dst_ap=dst_ap, nh=nh:
                     v.tensor_scalar(dst_ap[:, q * 512:(q + 1) * 512], PSA(comp),
                                     colG[:, nh:nh + 1], None, OP.mult),
                     reads=[RA(comp), ("colG", 0)], writes=[dregs[q]])
        if debug:
            b.dma("sync", lambda s, nh=nh: s.dma_start(
                out=tapBu[nh][:, 0:2048], in_=bre_ap), reads=BRE, q="Dout")
            b.dma("sync", lambda s, nh=nh: s.dma_start(
                out=tapBu[nh][:, 2048:4096], in_=bim_ap), reads=BIM, q="Dout")
        b.op("vector", lambda v: v.tensor_tensor(za, bre_ap, cos_ap, OP.mult),
             reads=BRE + COS, writes=ZA)
        b.op("gpsimd", lambda g: g.tensor_tensor(zb, bim_ap, sin_ap, OP.mult),
             reads=BIM + SIN, writes=ZB)
        b.op("vector", lambda v: v.tensor_tensor(za, za, zb, OP.add),
             reads=ZA + ZB, writes=ZA)
        b.op("gpsimd", lambda g: g.tensor_tensor(zb, bim_ap, cos_ap, OP.mult),
             reads=BIM + COS, writes=ZB)
        b.op("vector", lambda v: v.tensor_tensor(bre_ap, bre_ap, sin_ap, OP.mult),
             reads=BRE + SIN, writes=BRE)
        b.op("vector", lambda v: v.tensor_tensor(zb, zb, bre_ap, OP.subtract),
             reads=ZB + BRE, writes=ZB)
        b.op("vector", lambda v, nh=nh: v.tensor_tensor_scan(
            bre_ap, colM[:, nh:nh + 1].broadcast_to([128, 2048]), za,
            0.0, OP.mult, OP.add),
            reads=ZA + [("colM", 0)], writes=BRE)
        b.op("vector", lambda v, nh=nh: v.tensor_tensor_scan(
            bim_ap, colM[:, nh:nh + 1].broadcast_to([128, 2048]), zb,
            0.0, OP.mult, OP.add),
            reads=ZB + [("colM", 0)], writes=BIM)
        b.op("vector", lambda v: v.tensor_tensor(za, bre_ap, cos_ap, OP.mult),
             reads=BRE + COS, writes=ZA)
        b.op("gpsimd", lambda g: g.tensor_tensor(zb, bim_ap, sin_ap, OP.mult),
             reads=BIM + SIN, writes=ZB)
        b.op("vector", lambda v: v.tensor_tensor(za, za, zb, OP.subtract),
             reads=ZA + ZB, writes=ZA)
        b.op("gpsimd", lambda g: g.tensor_tensor(zb, bim_ap, cos_ap, OP.mult),
             reads=BIM + COS, writes=ZB)
        b.op("vector", lambda v: v.tensor_tensor(bre_ap, bre_ap, sin_ap, OP.mult),
             reads=BRE + SIN, writes=BRE)
        b.op("vector", lambda v: v.tensor_tensor(zb, zb, bre_ap, OP.add),
             reads=ZB + BRE, writes=ZB)
        b.op("scalar", lambda s, nh=nh: s.copy(HBF[:, nh * T:(nh + 1) * T], za),
             reads=ZA, writes=R("HBF", nh * T, (nh + 1) * T, es=2))
        b.op("scalar", lambda s, nh=nh: s.copy(HBF2[:, nh * T:(nh + 1) * T], zb),
             reads=ZB, writes=R("HBF2", nh * T, (nh + 1) * T, es=2))
        if debug and nh == 1:
            b.dma("sync", lambda s: s.dma_start(out=tapH[0], in_=HBF[:]),
                  reads=R("HBF", 0, 4096, es=2), q="Dout")
            b.dma("sync", lambda s: s.dma_start(out=tapH[1], in_=HBF2[:]),
                  reads=R("HBF2", 0, 4096, es=2), q="Dout")
        for ci in range(24 + nh * 13, 24 + (nh + 1) * 13):
            emit_abs_chunk(wgv_chunk(ci), ci)

    MX1, MN1 = MXA[:, 0:512], MXA[:, 512:1024]
    RMX1, RMN1 = [("MXA", 0)], [("MXA", 1)]

    def emit_xload(q, lo=0, n=16):
        for part in range(lo // 4, (lo + n) // 4):
            pl = part * 4
            b.dma("sync", lambda s, q=q, pl=pl: s.dma_start(
                out=P2.ap().rearrange("p (a t) -> p a t", a=DT)[:, pl:pl + 4, :],
                in_=xT_h.rearrange("(a p) t -> p a t", p=128)[:, pl:pl + 4,
                                                              q * 512:(q + 1) * 512]),
                writes=R("P2", pl * 512, (pl + 4) * 512))

    def emit_mix_a(q, a):
        bank = a % 2
        for im in range(2):
            for nh in range(2):
                CT = CBF if nh == 0 else CBF2
                cn = "CBF" if nh == 0 else "CBF2"
                HT = HBF if im == 0 else HBF2
                hn = "HBF" if im == 0 else "HBF2"
                b.op("tensor", lambda t, q=q, a=a, im=im, nh=nh, bank=bank,
                     CT=CT, HT=HT: t.matmul(
                    PSA(bank), CT[:, im * D + a * 128:im * D + (a + 1) * 128],
                    HT[:, nh * T + q * 512:nh * T + (q + 1) * 512],
                    start=(im == 0 and nh == 0), stop=(im == 1 and nh == 1)),
                    reads=R(cn, im * D + a * 128, im * D + (a + 1) * 128, es=2) +
                          R(hn, nh * T + q * 512, nh * T + (q + 1) * 512, es=2),
                    writes=[RA(bank)], inc=(im == 1 and nh == 1))
        b.op("vector", lambda v, a=a, bank=bank: v.scalar_tensor_tensor(
            P2[:, a * 512:(a + 1) * 512], P2[:, a * 512:(a + 1) * 512],
            colD[:, a:a + 1], PSA(bank), OP.mult, OP.add),
            reads=R("P2", a * 512, (a + 1) * 512) + [("colD", 0), RA(bank)],
            writes=R("P2", a * 512, (a + 1) * 512))
        b.op("tensor", lambda t, a=a: t.matmul(
            psB[0:1, 1024:1536], c_onesc[:], P2[:, a * 512:(a + 1) * 512],
            start=(a == 0), stop=(a == DT - 1)),
            reads=R("P2", a * 512, (a + 1) * 512) + [("c_onesc", 0)],
            writes=[RB(2)] if a in (0, DT - 1) else [], inc=True)
        sq = a % 2
        b.op("scalar", lambda s, a=a, sq=sq: s.activation(
            Z2S[:, sq * 512:(sq + 1) * 512], P2[:, a * 512:(a + 1) * 512],
            AF.Square),
            reads=R("P2", a * 512, (a + 1) * 512),
            writes=R("Z2S", sq * 512, (sq + 1) * 512))
        b.op("tensor", lambda t, a=a, sq=sq: t.matmul(
            psB[0:1, 1536:2048], c_onesc[:], Z2S[:, sq * 512:(sq + 1) * 512],
            start=(a == 0), stop=(a == DT - 1)),
            reads=R("Z2S", sq * 512, (sq + 1) * 512) + [("c_onesc", 0)],
            writes=[RB(3)] if a in (0, DT - 1) else [], inc=True)

    emit_xload(0)
    for a in range(DT):
        emit_mix_a(0, a)
    for q in range(NQ):
        b.op("vector", lambda v: v.tensor_scalar(
            repr_(0), psB[0:1, 1024:1536], 1.0 / D, None, OP.mult),
            reads=[RB(2)], writes=[("REP", 0)])
        b.op("vector", lambda v: v.tensor_scalar(
            rs(1), psB[0:1, 1536:2048], 1.0 / D, None, OP.mult),
            reads=[RB(3)], writes=R("rows", 512, 1024))
        b.op("vector", lambda v: v.tensor_tensor(rs(0), repr_(0), repr_(0), OP.mult),
             reads=[("REP", 0)], writes=R("rows", 0, 512))
        b.op("vector", lambda v: v.tensor_tensor(rs(1), rs(1), rs(0), OP.subtract),
             reads=R("rows", 0, 1024), writes=R("rows", 512, 1024))
        b.op("scalar", lambda s: s.activation(repr_(1), rs(1), AF.Sqrt,
                                              bias=srow[0:1, 4:5]),
             reads=R("rows", 512, 1024) + [("srow", 0)], writes=[("REP", 1)])
        b.op("vector", lambda v: v.reciprocal(repr_(1), repr_(1)),
             reads=[("REP", 1)], writes=[("REP", 1)])
        b.op("vector", lambda v: v.tensor_tensor(rs(2), repr_(1), repr_(1), OP.mult),
             reads=[("REP", 1)], writes=R("rows", 1024, 1536))
        b.op("vector", lambda v: v.tensor_tensor(rs(2), rs(2), rs(1), OP.mult),
             reads=R("rows", 512, 1536), writes=R("rows", 1024, 1536))
        b.op("scalar", lambda s: s.activation(rs(2), rs(2), AF.Sqrt,
                                              bias=srow[0:1, 4:5]),
             reads=R("rows", 1024, 1536) + [("srow", 0)], writes=R("rows", 1024, 1536))
        b.op("vector", lambda v: v.reciprocal(rs(2), rs(2)),
             reads=R("rows", 1024, 1536), writes=R("rows", 1024, 1536))
        bcast_row(repr_(0), [("REP", 0)], PSA(2), RA(2), REP[:, 0:512], [("REP", 0)])
        bcast_row(repr_(1), [("REP", 1)], PSA(3), RA(3), REP[:, 512:1024],
                  [("REP", 1)])
        for a in range(DT):
            k = a % 2
            b.op("vector", lambda v, a=a, k=k: v.tensor_tensor(
                OTS[:, k * 512:(k + 1) * 512], P2[:, a * 512:(a + 1) * 512],
                REP[:, 0:512], OP.subtract),
                reads=R("P2", a * 512, (a + 1) * 512) + [("REP", 0)],
                writes=[("OTS", k)])
            b.op("vector", lambda v, k=k: v.tensor_tensor(
                OTS[:, k * 512:(k + 1) * 512], OTS[:, k * 512:(k + 1) * 512],
                REP[:, 512:1024], OP.mult),
                reads=[("OTS", k), ("REP", 1)], writes=[("OTS", k)])
            b.dma("gpsimd", lambda g, q=q, a=a, k=k: g.dma_start(
                out=x1sp_t[:, a:a + 1, q * 512:(q + 1) * 512],
                in_=OTS.ap().rearrange("p (s t) -> p s t", s=2)[:, k:k + 1, :]),
                reads=[("OTS", k)], writes=[("x1sp", a, q)])
            if a == 0:
                b.op("vector", lambda v, a=a, k=k: v.tensor_scalar(
                    MX1, OTS[:, k * 512:(k + 1) * 512], colNW[:, a:a + 1], None,
                    OP.mult),
                    reads=[("OTS", k), ("colNW", 0)], writes=RMX1)
                b.op("vector", lambda v, a=a, k=k: v.tensor_scalar(
                    MN1, OTS[:, k * 512:(k + 1) * 512], colNW[:, a:a + 1], None,
                    OP.mult),
                    reads=[("OTS", k), ("colNW", 0)], writes=RMN1)
            else:
                b.op("vector", lambda v, a=a, k=k: v.scalar_tensor_tensor(
                    MX1, OTS[:, k * 512:(k + 1) * 512], colNW[:, a:a + 1], MX1,
                    OP.mult, OP.max),
                    reads=[("OTS", k), ("colNW", 0)] + RMX1, writes=RMX1)
                b.op("vector", lambda v, a=a, k=k: v.scalar_tensor_tensor(
                    MN1, OTS[:, k * 512:(k + 1) * 512], colNW[:, a:a + 1], MN1,
                    OP.mult, OP.min),
                    reads=[("OTS", k), ("colNW", 0)] + RMN1, writes=RMN1)
            if q + 1 < NQ:
                if a % 4 == 3:
                    emit_xload(q + 1, a - 3, 4)
                if a >= 4:
                    emit_mix_a(q + 1, a - 4)
        b.op("vector", lambda v: v.tensor_scalar(MN1, MN1, -1.0, None, OP.mult),
             reads=RMN1, writes=RMN1)
        b.op("vector", lambda v: v.tensor_tensor(MX1, MX1, MN1, OP.max),
             reads=RMX1 + RMN1, writes=RMX1)
        for j in range(4):
            b.op("tensor", lambda t, j=j: t.transpose(
                PSB(1)[:, 0:128], MXA[:, j * 128:(j + 1) * 128], c_id[:]),
                reads=RMX1 + [("c_id", 0)], writes=[RB(1)])
            b.op("vector", lambda v, j=j: v.reduce_max(
                MXA[:, 1024 + j:1025 + j], PSB(1)[:, 0:128], axis=AX.X),
                reads=[RB(1)], writes=[("MXA", 2)])
        b.op("tensor", lambda t: t.transpose(
            PSB(1)[0:4, 0:128], MXA[:, 1024:1028], c_id[:]),
            reads=[("MXA", 2), ("c_id", 0)], writes=[RB(1)])
        b.op("vector", lambda v: v.tensor_copy(MXA[0:4, 1056:1184],
                                               PSB(1)[0:4, 0:128]),
             reads=[RB(1)], writes=[("MXA", 2)])
        b.dma("gpsimd", lambda g: g.dma_start(
            out=rs(0).rearrange("e (j p) -> e j p", j=4),
            in_=MXA[0:4, 1056:1184]),
            reads=[("MXA", 2)], writes=R("rows", 0, 512))
        b.op("vector", lambda v: v.tensor_tensor(rs(0), rs(0), rs(2), OP.mult),
             reads=R("rows", 0, 512) + R("rows", 1024, 1536),
             writes=R("rows", 0, 512))
        b.op("vector", lambda v: v.tensor_scalar(rs(0), rs(0), 1e-5, None, OP.max),
             reads=R("rows", 0, 512), writes=R("rows", 0, 512))
        b.op("vector", lambda v: v.reciprocal(rs(1), rs(0)),
             reads=R("rows", 0, 512), writes=R("rows", 512, 1024))
        b.op("vector", lambda v: v.tensor_scalar(rs(1), rs(1), QB, None, OP.mult),
             reads=R("rows", 512, 1024), writes=R("rows", 512, 1024))
        bcast_row(rs(0), R("rows", 0, 512), PSA(3), RA(3),
                  dqa1rep[:, q * 512:(q + 1) * 512],
                  R("P3", q * 512, (q + 1) * 512), scale=1.0 / QB)
        b.op("vector", lambda v: v.tensor_tensor(rs(2), rs(2), rs(1), OP.mult),
             reads=R("rows", 512, 1536), writes=R("rows", 1024, 1536))
        bcast_row(rs(2), R("rows", 1024, 1536), PSA(2), RA(2),
                  SREP, [("MXA", 3)])
        for a in range(DT):
            k = a % 2
            b.dma("sync", lambda s, q=q, a=a, k=k: s.dma_start(
                out=OTS.ap().rearrange("p (s t) -> p s t", s=2)[:, k:k + 1, :],
                in_=x1sp_t[:, a:a + 1, q * 512:(q + 1) * 512]),
                reads=[("x1sp", a, q)], writes=[("OTS", k)])
            src_ap, src_regs = RN[k][0], RN[k][1]
            b.op("vector", lambda v, a=a, k=k, src_ap=src_ap: v.scalar_tensor_tensor(
                src_ap, OTS[:, k * 512:(k + 1) * 512], colNW[:, a:a + 1], SREP,
                OP.mult, OP.mult),
                reads=[("OTS", k), ("colNW", 0), ("MXA", 3)], writes=src_regs)
            round_neg(k, P1[:, a * T + q * 512:a * T + (q + 1) * 512],
                      R("P1", a * T + q * 512, a * T + (q + 1) * 512, es=2))
            if q + 1 < NQ and a < 4:
                emit_mix_a(q + 1, 12 + a)
        for ci in range(50 + q * 20, min(NGV, 70 + q * 20)):
            emit_abs_chunk(wgv_chunk(ci), ci)

    b.op("vector", lambda v: v.reduce_sum(
        MXA[0:128, 0:1], slots[:, 0:128], axis=AX.X),
        reads=[("slots", i) for i in range(NGV)], writes=[("MXA", 0)])
    b.op("tensor", lambda t: t.matmul(
        psB[0:1, 1024:1025], c_onesc[:], MXA[0:128, 0:1], start=True, stop=True),
        reads=[("MXA", 0), ("c_onesc", 0)], writes=[RB(2)])
    b.op("vector", lambda v: v.tensor_scalar(
        srow[0:1, 0:1], psB[0:1, 1024:1025], 1.0 / (D * 2.0 * OH), None, OP.mult),
        reads=[RB(2)], writes=[("srow", 1)])
    b.op("vector", lambda v: v.tensor_scalar(
        srow[0:1, 0:1], srow[0:1, 0:1], 1e-5, None, OP.max),
        reads=[("srow", 1)], writes=[("srow", 1)])
    b.op("tensor", lambda t: t.matmul(
        psB[:, 1024:1025], c_ones[:, 0:128], srow[0:1, 0:1], start=True, stop=True),
        reads=[("srow", 1), ("c_ones", 0)], writes=[RB(2)])
    b.op("vector", lambda v: v.tensor_scalar(
        colS[:, 0:1], psB[:, 1024:1025], 0.5, None, OP.mult),
        reads=[RB(2)], writes=[("colS", 0)])
    b.op("vector", lambda v: v.tensor_scalar(
        colS[:, 1:2], psB[:, 1024:1025], -0.5, None, OP.mult),
        reads=[RB(2)], writes=[("colS", 0)])
    b.op("vector", lambda v: v.tensor_scalar(
        colSG[:, 0:1], psB[:, 1024:1025], -0.5, None, OP.mult),
        reads=[RB(2)], writes=[("colSG", 0)])
    b.op("vector", lambda v: v.tensor_scalar(
        dqa1rep[:], dqa1rep[:], colSG[:, 0:1], None, OP.mult),
        reads=R("P3", 0, 2048) + [("colSG", 0)], writes=R("P3", 0, 2048))
    if debug:
        b.dma("sync", lambda s: s.dma_start(out=tapDQ[:], in_=dqa1rep[:]),
              reads=R("P3", 0, 2048), q="Dout")

    def wload(p, side, sl):
        blk = side * 64 + p
        b.dma("sync", lambda s, blk=blk, sl=sl: s.dma_start(
            out=WV[:, sl * 2048:(sl + 1) * 2048].rearrange("p (a o) -> p a o", a=DT),
            in_=wgvL_h[blk]),
            writes=R("P2", sl * 2048, (sl + 1) * 2048))

    def wquant(sl, alt):
        SGa = HBF[:, alt * 2048:(alt + 1) * 2048]
        SGar = R("HBF", alt * 2048, (alt + 1) * 2048, es=2)
        b.op("scalar", lambda s, sl=sl: s.activation(
            WQap(sl), WV[:, sl * 2048:(sl + 1) * 2048], AF.Sign, bias=colS[:, 1:2]),
            reads=R("P2", sl * 2048, (sl + 1) * 2048) + [("colS", 0)],
            writes=WQreg(sl))
        b.op("scalar", lambda s, sl=sl, SGa=SGa: s.activation(
            SGa, WV[:, sl * 2048:(sl + 1) * 2048], AF.Sign, bias=colS[:, 0:1]),
            reads=R("P2", sl * 2048, (sl + 1) * 2048) + [("colS", 0)],
            writes=SGar)
        b.op("vector", lambda v, sl=sl, SGa=SGa: v.tensor_tensor(
            WQap(sl), WQap(sl), SGa, OP.add),
            reads=WQreg(sl) + SGar, writes=WQreg(sl))

    def emit_wout_prep(j):
        b.dma("scalar", lambda s, j=j: s.dma_start(out=WOF[:], in_=wout_t[:, j, :]),
              writes=R("WOF", 0, 2048), q="Dscalar")
        b.op("scalar", lambda s: s.activation(WOQ[:, 0:2048], WOF[:], AF.Sign,
                                              bias=colS2[:, 1:2]),
             reads=R("WOF", 0, 2048) + [("colS2", 0)], writes=R("P3", 2048, 3072))
        b.op("scalar", lambda s: s.activation(WR[:, 2048:4096], WOF[:], AF.Sign,
                                              bias=colS2[:, 0:1]),
             reads=R("WOF", 0, 2048) + [("colS2", 0)],
             writes=R("WR", 2048, 4096, es=2))
        b.op("vector", lambda v: v.tensor_tensor(
            WOQ[:, 0:2048], WOQ[:, 0:2048], WR[:, 2048:4096], OP.add),
            reads=R("P3", 2048, 3072) + R("WR", 2048, 4096, es=2),
            writes=R("P3", 2048, 3072))
        b.dma("gpsimd", lambda g, j=j: g.dma_start(
            out=wq2sp[:, j // 8, :, j % 8, :].rearrange("g p c -> p g c"),
            in_=WOQ[:, 0:2048].rearrange("p (g c) -> p g c", g=8)),
            reads=R("P3", 2048, 3072), writes=[("wq2sp", j)])

    wload(0, 0, 0)
    wload(0, 1, 1)
    wquant(0, 0)
    wquant(1, 1)
    woj = [0]

    for p in range(PAIRS):
        sv = (p % 2) * 2
        so = ((p + 1) % 2) * 2
        if p + 1 < PAIRS:
            wload(p + 1, 0, so)
            wload(p + 1, 1, so + 1)
        for th in range(2):
            if th == 1 and p + 1 < PAIRS:
                wquant(so, 0)
                wquant(so + 1, 1)
            PS = psA if th == 0 else psB
            RR = RA if th == 0 else RB
            for vg in range(2):
                for a in range(DT):
                    for tq in range(2):
                        last = (a == DT - 1)
                        b.op("tensor", lambda t, PS=PS, vg=vg, a=a, tq=tq, th=th,
                             sv=sv, last=last: t.matmul(
                            PS[:, (vg * 2 + tq) * 512:(vg * 2 + tq + 1) * 512],
                            WQap(sv + vg)[:, a * 128:(a + 1) * 128],
                            P1[:, a * T + th * 1024 + tq * 512:
                               a * T + th * 1024 + (tq + 1) * 512],
                            start=(a == 0), stop=last),
                            reads=WQreg(sv + vg) +
                                  R("P1", a * T + th * 1024 + tq * 512,
                                    a * T + th * 1024 + (tq + 1) * 512, es=2),
                            writes=[RR(vg * 2 + tq)], inc=last)
            hsl = th
            h2regs = R("CBF", 2048 + hsl * 1024, 2048 + (hsl + 1) * 1024, es=2)
            b.op("vector", lambda v, PS=PS, th=th: v.tensor_tensor(
                GS[:, 0:1024], PS[:, 1024:2048],
                dqa1rep[:, th * 1024:(th + 1) * 1024], OP.mult),
                reads=[RR(2), RR(3)] + R("P3", th * 1024, (th + 1) * 1024),
                writes=R("HBF2", 0, 2048, es=2))
            b.op("vector", lambda v, PS=PS, th=th: v.tensor_tensor(
                T2[:, 0:1024], PS[:, 0:1024],
                dqa1rep[:, th * 1024:(th + 1) * 1024], OP.mult),
                reads=[RR(0), RR(1)] + R("P3", th * 1024, (th + 1) * 1024),
                writes=R("CBF", 0, 2048, es=2))
            b.op("scalar", lambda s: s.activation(SIL[:, 0:1024], GS[:, 0:1024],
                                                  AF.Silu),
                 reads=R("HBF2", 0, 2048, es=2), writes=R("HBF2", 2048, 4096, es=2))
            b.op("vector", lambda v, hsl=hsl: v.tensor_tensor(
                H2T[:, hsl * 1024:(hsl + 1) * 1024], T2[:, 0:1024],
                SIL[:, 0:1024], OP.mult),
                reads=R("CBF", 0, 2048, es=2) + R("HBF2", 2048, 4096, es=2),
                writes=h2regs)
            if p == 0:
                b.op("vector", lambda v, th=th, hsl=hsl: v.tensor_copy(
                    MXB[:, th * 1024:(th + 1) * 1024],
                    H2T[:, hsl * 1024:(hsl + 1) * 1024]),
                    reads=h2regs, writes=[("MXA", th)])
                b.op("vector", lambda v, th=th, hsl=hsl: v.tensor_copy(
                    MNB[:, th * 1024:(th + 1) * 1024],
                    H2T[:, hsl * 1024:(hsl + 1) * 1024]),
                    reads=h2regs, writes=[("MXA", 2 + th)])
            else:
                b.op("vector", lambda v, th=th, hsl=hsl: v.tensor_tensor(
                    MXB[:, th * 1024:(th + 1) * 1024],
                    MXB[:, th * 1024:(th + 1) * 1024],
                    H2T[:, hsl * 1024:(hsl + 1) * 1024], OP.max),
                    reads=[("MXA", th)] + h2regs, writes=[("MXA", th)])
                b.op("vector", lambda v, th=th, hsl=hsl: v.tensor_tensor(
                    MNB[:, th * 1024:(th + 1) * 1024],
                    MNB[:, th * 1024:(th + 1) * 1024],
                    H2T[:, hsl * 1024:(hsl + 1) * 1024], OP.min),
                    reads=[("MXA", 2 + th)] + h2regs, writes=[("MXA", 2 + th)])
            b.dma("gpsimd", lambda g, p=p, th=th, hsl=hsl: g.dma_start(
                out=h2sp_t[:, p, th * 1024:(th + 1) * 1024],
                in_=H2T[:, hsl * 1024:(hsl + 1) * 1024]),
                reads=h2regs, writes=[("h2sp", p, th)])
        if p < 16:
            for ii in range(4):
                emit_abs_chunk(wout_chunk(4 * p + ii), NGV + 4 * p + ii)
        elif p == 16:
            b.op("vector", lambda v: v.reduce_sum(
                Z2S[0:128, 0:1], slots[:, NGV:NGV + NWO], axis=AX.X),
                reads=[("slots", NGV + i) for i in range(NWO)],
                writes=R("Z2S", 0, 512))
            b.op("tensor", lambda t: t.matmul(
                psB[0:1, 1536:1537], c_onesc[:], Z2S[0:128, 0:1],
                start=True, stop=True),
                reads=R("Z2S", 0, 512) + [("c_onesc", 0)], writes=[RB(3)])
            b.op("vector", lambda v: v.tensor_scalar(
                srow[0:1, 2:3], psB[0:1, 1536:1537], 1.0 / (D * float(OH)), None,
                OP.mult),
                reads=[RB(3)], writes=[("srow", 2)])
            b.op("vector", lambda v: v.tensor_scalar(
                srow[0:1, 2:3], srow[0:1, 2:3], 1e-5, None, OP.max),
                reads=[("srow", 2)], writes=[("srow", 2)])
            b.op("vector", lambda v: v.tensor_scalar(
                srow[0:1, 3:4], srow[0:1, 2:3], -0.5, None, OP.mult),
                reads=[("srow", 2)], writes=[("srow", 3)])
            b.op("tensor", lambda t: t.matmul(
                psB[:, 1536:1537], c_ones[:, 0:128], srow[0:1, 2:3],
                start=True, stop=True),
                reads=[("srow", 2), ("c_ones", 0)], writes=[RB(3)])
            b.op("vector", lambda v: v.tensor_scalar(
                colS2[:, 0:1], psB[:, 1536:1537], 0.5, None, OP.mult),
                reads=[RB(3)], writes=[("colS2", 0)])
            b.op("vector", lambda v: v.tensor_scalar(
                colS2[:, 1:2], psB[:, 1536:1537], -0.5, None, OP.mult),
                reads=[RB(3)], writes=[("colS2", 0)])
        elif woj[0] < PAIRS:
            for _ in range(2):
                if woj[0] < PAIRS:
                    emit_wout_prep(woj[0])
                    woj[0] += 1
    while woj[0] < PAIRS:
        emit_wout_prep(woj[0])
        woj[0] += 1

    RMXall = [("MXA", i) for i in range(4)]
    RMNall = [("MNA", i) for i in range(4)]
    b.op("vector", lambda v: v.tensor_scalar(
        MNB[:, 0:2048], MNB[:, 0:2048], -1.0, None, OP.mult),
        reads=[("MXA", 2), ("MXA", 3)], writes=[("MXA", 2), ("MXA", 3)])
    b.op("vector", lambda v: v.tensor_tensor(
        MXB[:, 0:2048], MXB[:, 0:2048], MNB[:, 0:2048], OP.max),
        reads=RMXall, writes=[("MXA", 0), ("MXA", 1)])
    for j in range(16):
        b.op("tensor", lambda t, j=j: t.transpose(
            PSB(1).bitcast(BF16)[:, 0:128], MXB[:, j * 128:(j + 1) * 128],
            c_idb[:]),
            reads=[("MXA", 0), ("MXA", 1), ("c_idb", 0)], writes=[RB(1)])
        b.op("vector", lambda v, j=j: v.reduce_max(
            MNA[:, j:j + 1], PSB(1).bitcast(BF16)[:, 0:128], axis=AX.X),
            reads=[RB(1)], writes=[("MNA", 0)])
    b.op("tensor", lambda t: t.transpose(
        PSB(1)[0:16, 0:128], MNA[:, 0:16], c_id[:]),
        reads=[("MNA", 0), ("c_id", 0)], writes=[RB(1)])
    b.op("vector", lambda v: v.tensor_copy(MNA[0:16, 64:192], PSB(1)[0:16, 0:128]),
         reads=[RB(1)], writes=[("MNA", 0)])
    b.dma("gpsimd", lambda g: g.dma_start(
        out=r_amax2.rearrange("e (j p) -> e j p", j=16),
        in_=MNA[0:16, 64:192]),
        reads=[("MNA", 0)], writes=R("P3", 0, 2048))
    b.op("vector", lambda v: v.tensor_scalar(
        r_amax2, r_amax2, 1e-5, None, OP.max),
        reads=R("P3", 0, 2048), writes=R("P3", 0, 2048))
    if debug:
        b.dma("sync", lambda s: s.dma_start(out=tapAM[:], in_=r_amax2),
              reads=R("P3", 0, 2048), q="Dout")

    S2B = [REP[:, 0:512], MXA[:, 0:512]]
    S2R = [[("REP", 0)], [("MXA", 0)]]
    DQB = [REP[:, 512:1024], MXA[:, 512:1024]]
    DQR = [[("REP", 1)], [("MXA", 1)]]
    M2B, M2R = MXA[:, 1024:1536], [("MXA", 2)]
    m2row = M2B[0:1, 0:512]

    def d_rowsQ(q):
        par = q % 2
        s2row = S2B[par][0:1, 0:512]
        dqrow = DQB[par][0:1, 0:512]
        b.op("vector", lambda v, q=q, s2row=s2row: v.reciprocal(
            s2row, r_amax2[0:1, q * 512:(q + 1) * 512]),
            reads=R("P3", 0, 2048), writes=S2R[par])
        b.op("vector", lambda v, s2row=s2row: v.tensor_scalar(
            s2row, s2row, QB, None, OP.mult),
            reads=S2R[par], writes=S2R[par])
        b.op("vector", lambda v, q=q, dqrow=dqrow: v.tensor_scalar(
            dqrow, r_amax2[0:1, q * 512:(q + 1) * 512], 1.0 / QB, None, OP.mult),
            reads=R("P3", 0, 2048), writes=DQR[par])
        b.op("vector", lambda v, dqrow=dqrow: v.tensor_scalar(
            dqrow, dqrow, srow[0:1, 3:4], None, OP.mult),
            reads=DQR[par] + [("srow", 3)], writes=DQR[par])
        bcast_row(s2row, S2R[par], PSB(2), RB(2), S2B[par], S2R[par])
        bcast_row(dqrow, DQR[par], PSB(3), RB(3), DQB[par], DQR[par])

    RNW = [
        (MNA[:, 0:1024], R("MNA", 0, 1024),
         MNA[:, 1024:2048].bitcast(I32), MNA[:, 1024:2048], R("MNA", 1024, 2048)),
        (WOF[:, 0:1024], R("WOF", 0, 1024),
         WOF[:, 1024:2048].bitcast(I32), WOF[:, 1024:2048], R("WOF", 1024, 2048)),
    ]

    def d_chain(q, seg, j):
        par = q % 2
        src_ap, src_r, i_ap, cf_ap, ic_r = RNW[(j // 2) % 2]
        for h in range(2):
            jo = seg * 32 + j + h
            hq = jo % 8
            b.dma("sync", lambda s, q=q, jo=jo, hq=hq: s.dma_start(
                out=CBF[:, hq * 512:(hq + 1) * 512],
                in_=h2sp_t[:, jo, q * 512:(q + 1) * 512]),
                reads=[("h2sp", jo, q // 2)],
                writes=R("CBF", hq * 512, (hq + 1) * 512, es=2))
            b.op("vector", lambda v, hq=hq, h=h, src_ap=src_ap, par=par:
                 v.tensor_tensor(
                src_ap[:, h * 512:(h + 1) * 512],
                CBF[:, hq * 512:(hq + 1) * 512], S2B[par], OP.mult),
                reads=R("CBF", hq * 512, (hq + 1) * 512, es=2) + S2R[par],
                writes=src_r)
        dst_ap = P1[:, seg * 16384 + j * 512:seg * 16384 + (j + 2) * 512]
        dst_r = R("P1", seg * 16384 + j * 512, seg * 16384 + (j + 2) * 512, es=2)
        b.op("vector", lambda v: v.tensor_copy(i_ap, src_ap),
             reads=src_r, writes=ic_r)
        b.op("vector", lambda v: v.tensor_copy(cf_ap, i_ap),
             reads=ic_r, writes=ic_r)
        b.op("vector", lambda v: v.tensor_tensor(src_ap, src_ap, cf_ap,
                                                 OP.subtract),
             reads=src_r + ic_r, writes=src_r)
        b.op("vector", lambda v: v.scalar_tensor_tensor(
            cf_ap, src_ap, 0.5, cf_ap, OP.is_ge, OP.add),
            reads=src_r + ic_r, writes=ic_r)
        b.op("vector", lambda v, dst_ap=dst_ap: v.scalar_tensor_tensor(
            dst_ap, src_ap, -0.5, cf_ap, OP.is_le, OP.subtract),
            reads=src_r + ic_r, writes=dst_r)

    def d_quant(q, seg):
        for j in range(0, 32, 2):
            d_chain(q, seg, j)

    def d_mm(q, seg, feeder=()):
        feeder = list(feeder)
        fi = 0
        par = q % 2
        late = []
        for dg in range(8):
            d0 = dg * 2
            rot = dg % 3
            PSm = [lambda k: PSA(k), lambda k: PSA(2 + k), lambda k: PSB(k)][rot]
            RRm = [lambda k: RA(k), lambda k: RA(2 + k), lambda k: RB(k)][rot]
            for blk in range(4):
                gj = seg * 32 + blk * 8
                sl = blk % 2
                b.dma("gpsimd", lambda g, dg=dg, gj=gj, sl=sl: g.dma_start(
                    out=WR[:, sl * 2048:(sl + 1) * 2048].rearrange(
                        "p (j c) -> p j c", j=8),
                    in_=wq2sp[dg, gj // 8]),
                    reads=[("wq2sp", j) for j in range(gj, gj + 8)],
                    writes=R("WR", sl * 2048, (sl + 1) * 2048, es=2),
                    q="Dwq")
                for jj in range(8):
                    jl = blk * 8 + jj
                    for ds in range(2):
                        b.op("tensor", lambda t, jl=jl, jj=jj, ds=ds, sl=sl,
                             seg=seg, PSm=PSm: t.matmul(
                            PSm(ds),
                            WR[:, sl * 2048 + jj * 256 + ds * 128:
                               sl * 2048 + jj * 256 + (ds + 1) * 128],
                            P1[:, seg * 16384 + jl * 512:
                               seg * 16384 + (jl + 1) * 512],
                            start=(jl == 0), stop=(jl == 31)),
                            reads=R("WR", sl * 2048, (sl + 1) * 2048, es=2) +
                                  R("P1", seg * 16384 + jl * 512,
                                    seg * 16384 + (jl + 1) * 512, es=2),
                            writes=[RRm(ds)] if jl in (0, 31) else [],
                            inc=((jj == 7 and ds == 1) or jl == 31))
            for f in late:
                f()
            late = []
            if seg == 0:
                for ds in range(2):
                    t_ = dg * 2 + ds
                    b.op("scalar", lambda s, ds=ds, t_=t_, PSm=PSm: s.copy(
                        P2[:, t_ * 512:(t_ + 1) * 512], PSm(ds)),
                        reads=[RRm(ds)], writes=R("P2", t_ * 512, (t_ + 1) * 512))
                take = -(-(len(feeder) - fi) // (8 - dg))
                for _ in range(take):
                    feeder[fi]()
                    fi += 1
                continue
            for ds in range(2):
                a = d0 + ds
                t_ = dg * 2 + ds
                b.dma("scalar", lambda s, q=q, a=a, ds=ds: s.dma_start(
                    out=X1S[:, ds * 512:(ds + 1) * 512],
                    in_=x1sp_t[:, a:a + 1, q * 512:(q + 1) * 512]),
                    reads=[("x1sp", a, q)],
                    writes=R("CBF2", ds * 1024, (ds + 1) * 1024, es=2),
                    q="Dscalar")
                b.op("vector", lambda v, ds=ds, t_=t_, PSm=PSm: v.tensor_tensor(
                    Z2S[:, ds * 512:(ds + 1) * 512], PSm(ds),
                    P2[:, t_ * 512:(t_ + 1) * 512], OP.add),
                    reads=[RRm(ds)] + R("P2", t_ * 512, (t_ + 1) * 512),
                    writes=R("Z2S", ds * 512, (ds + 1) * 512))
                b.op("gpsimd", lambda g, ds=ds, par=par: g.tensor_tensor(
                    Z2S[:, ds * 512:(ds + 1) * 512],
                    Z2S[:, ds * 512:(ds + 1) * 512], DQB[par], OP.mult),
                    reads=R("Z2S", ds * 512, (ds + 1) * 512) + DQR[par],
                    writes=R("Z2S", ds * 512, (ds + 1) * 512))
                b.op("vector", lambda v, ds=ds: v.scalar_tensor_tensor(
                    Z2S[:, (2 + ds) * 512:(3 + ds) * 512],
                    X1S[:, ds * 512:(ds + 1) * 512], 2.0,
                    Z2S[:, ds * 512:(ds + 1) * 512], OP.mult, OP.add),
                    reads=R("CBF2", ds * 1024, (ds + 1) * 1024, es=2) +
                          R("Z2S", ds * 512, (ds + 1) * 512),
                    writes=R("Z2S", (2 + ds) * 512, (3 + ds) * 512))
                b.dma("scalar", lambda s, q=q, a=a, ds=ds: s.dma_start(
                    out=z2sp_t[:, a:a + 1, q * 512:(q + 1) * 512],
                    in_=Z2S.ap().rearrange("p (s t) -> p s t", s=4)[:, 2 + ds, :]),
                    reads=R("Z2S", (2 + ds) * 512, (3 + ds) * 512),
                    writes=[("z2sp", a, q)], q="Dscalar")
                def mk_sum(ds=ds, dg=dg):
                    b.op("tensor", lambda t, ds=ds, dg=dg: t.matmul(
                        psB[0:1, 1536:2048], c_onesc[:],
                        Z2S[:, (2 + ds) * 512:(3 + ds) * 512],
                        start=(dg == 0 and ds == 0), stop=(dg == 7 and ds == 1)),
                        reads=R("Z2S", (2 + ds) * 512, (3 + ds) * 512) +
                              [("c_onesc", 0)],
                        writes=[RB(3)] if (dg, ds) in ((0, 0), (7, 1)) else [],
                        inc=True)
                late.append(mk_sum)
                sqo = 2048 + ds * 512
                b.op("scalar", lambda s, ds=ds, sqo=sqo: s.activation(
                    P3[:, sqo:sqo + 512], Z2S[:, (2 + ds) * 512:(3 + ds) * 512],
                    AF.Square),
                    reads=R("Z2S", (2 + ds) * 512, (3 + ds) * 512),
                    writes=R("P3", sqo, sqo + 512))

                def mk_sumsq(ds=ds, dg=dg, sqo=sqo):
                    b.op("tensor", lambda t, ds=ds, dg=dg, sqo=sqo: t.matmul(
                        psB[0:1, 1024:1536], c_onesc[:], P3[:, sqo:sqo + 512],
                        start=(dg == 0 and ds == 0), stop=(dg == 7 and ds == 1)),
                        reads=R("P3", sqo, sqo + 512) + [("c_onesc", 0)],
                        writes=[RB(2)] if (dg, ds) in ((0, 0), (7, 1)) else [],
                        inc=True)
                late.append(mk_sumsq)
            take = -(-(len(feeder) - fi) // (8 - dg))
            for _ in range(take):
                feeder[fi]()
                fi += 1
        for f in late:
            f()

    def d_rowsL(q):
        b.op("vector", lambda v: v.tensor_scalar(
            m2row, psB[0:1, 1536:2048], 1.0 / D, None, OP.mult),
            reads=[RB(3)], writes=M2R)
        b.op("vector", lambda v: v.tensor_scalar(
            rs(1), psB[0:1, 1024:1536], 1.0 / D, None, OP.mult),
            reads=[RB(2)], writes=R("rows", 512, 1024))
        b.op("vector", lambda v: v.tensor_tensor(rs(0), m2row, m2row, OP.mult),
             reads=M2R, writes=R("rows", 0, 512))
        b.op("vector", lambda v: v.tensor_tensor(rs(1), rs(1), rs(0), OP.subtract),
             reads=R("rows", 0, 1024), writes=R("rows", 512, 1024))
        b.op("scalar", lambda s: s.activation(srep_row, rs(1), AF.Sqrt,
                                              bias=srow[0:1, 4:5]),
             reads=R("rows", 512, 1024) + [("srow", 0)], writes=[("MXA", 3)])
        b.op("vector", lambda v: v.reciprocal(srep_row, srep_row),
             reads=[("MXA", 3)], writes=[("MXA", 3)])
        bcast_row(m2row, M2R, PSB(2), RB(2), M2B, M2R)
        bcast_row(srep_row, [("MXA", 3)], PSB(3), RB(3),
                  SREP, [("MXA", 3)])

    def d_apply_a(q, a):
        sl = a % 2
        b.dma("scalar", lambda s, q=q, a=a, sl=sl: s.dma_start(
            out=Z2S.ap().rearrange("p (s t) -> p s t", s=4)[:, sl, :],
            in_=z2sp_t[:, a:a + 1, q * 512:(q + 1) * 512]),
            reads=[("z2sp", a, q)], writes=R("Z2S", sl * 512, (sl + 1) * 512),
            q="Dscalar")
        b.op("gpsimd", lambda g, sl=sl: g.tensor_tensor(
            Z2S[:, (2 + sl) * 512:(3 + sl) * 512],
            Z2S[:, sl * 512:(sl + 1) * 512], M2B, OP.subtract),
            reads=R("Z2S", sl * 512, (sl + 1) * 512) + M2R,
            writes=R("Z2S", (2 + sl) * 512, (3 + sl) * 512))
        b.op("gpsimd", lambda g, sl=sl: g.tensor_tensor(
            OTS[:, sl * 512:(sl + 1) * 512],
            Z2S[:, (2 + sl) * 512:(3 + sl) * 512], SREP, OP.mult),
            reads=R("Z2S", (2 + sl) * 512, (3 + sl) * 512) + [("MXA", 3)],
            writes=[("OTS", sl)])
        b.dma("gpsimd", lambda g, q=q, a=a, sl=sl: g.dma_start(
            out=out_t[:, a:a + 1, q * 512:(q + 1) * 512],
            in_=OTS.ap().rearrange("p (s t) -> p s t", s=2)[:, sl, :]),
            reads=[("OTS", sl)], writes=[], q="Dout")

    d_rowsQ(0)
    d_quant(0, 0)
    tail = []
    for q in range(NQ):
        f0 = [(lambda q=q, j=j: d_chain(q, 1, j)) for j in range(0, 32, 2)] + tail
        if q + 1 < NQ:
            f0.append(lambda q=q: d_rowsQ(q + 1))
        d_mm(q, 0, f0)
        if debug and q == 0:
            b.dma("sync", lambda s: s.dma_start(out=tapA2[:], in_=P1[:]),
                  reads=R("P1", 0, DT * T, es=2), q="Dout")
        if q + 1 < NQ:
            f1 = [(lambda q=q, j=j: d_chain(q + 1, 0, j)) for j in range(0, 32, 2)]
        else:
            f1 = []
        d_mm(q, 1, f1)
        tail = ([(lambda q=q: d_rowsL(q))] +
                [(lambda q=q, a=a: d_apply_a(q, a)) for a in range(DT)])
    for f in tail:
        f()

    for qq in ("Dout", "Dsync", "Dgpsimd", "Dscalar", "Dwq"):
        for i in range(8):
            b._wait("sync", f"{qq}.{i}", b.cnt[f"{qq}.{i}"])
    for e in ENGINES[1:]:
        b._wait("sync", "E" + e, b.cnt["E" + e])

    import contextlib
    with contextlib.ExitStack() as stack:
        semh = {name: stack.enter_context(nc.semaphore(name)) for name in b.semnames}
        with nc.Block() as block:
            def emit(engobj, ename):
                for item in b.stream[ename]:
                    if item[0] == "w":
                        engobj.wait_ge(semh[item[1]], item[2])
                    else:
                        instr = item[1](engobj)
                        if item[2] is not None:
                            instr.then_inc(semh[item[2]], item[3])

            @block.sync
            def _(sync):
                emit(sync, "sync")

            @block.gpsimd
            def _(gp):
                emit(gp, "gpsimd")

            @block.vector
            def _(v):
                emit(v, "vector")

            @block.scalar
            def _(sc):
                emit(sc, "scalar")

            @block.tensor
            def _(t):
                emit(t, "tensor")

    return nc


_CACHE = {}


def kernel(L0_attn_x, L0_attn_nu_log, L0_attn_theta_log, L0_attn_B_re,
           L0_attn_B_im, L0_attn_C_re, L0_attn_C_im, L0_attn_D,
           L0_ffn_norm_w, L0_ffn_w_gv, L0_ffn_w_out):
    x = np.asarray(L0_attn_x)
    BATCH = x.shape[0]
    import os
    dbg = bool(os.environ.get("KDEBUG"))
    key = ("nc", dbg)
    if key not in _CACHE:
        _CACHE[key] = build_program(debug=dbg)
    nc = _CACHE[key]

    wgvT = np.asarray(L0_ffn_w_gv).T
    wgvL = np.ascontiguousarray(
        wgvT.reshape(DT, 128, 128, 128).transpose(2, 1, 0, 3))
    woutT = np.ascontiguousarray(np.asarray(L0_ffn_w_out).T)
    BreT = np.ascontiguousarray(np.asarray(L0_attn_B_re).T)
    BimT = np.ascontiguousarray(np.asarray(L0_attn_B_im).T)
    CreT = np.ascontiguousarray(np.asarray(L0_attn_C_re).T)
    CimT = np.ascontiguousarray(np.asarray(L0_attn_C_im).T)
    nul = np.ascontiguousarray(np.asarray(L0_attn_nu_log).reshape(1, NS))
    thl = np.ascontiguousarray(np.asarray(L0_attn_theta_log).reshape(1, NS))
    Dv = np.ascontiguousarray(np.asarray(L0_attn_D).reshape(1, D))
    nw = np.ascontiguousarray(np.asarray(L0_ffn_norm_w).reshape(1, D))

    in_maps = []
    for bb in range(BATCH):
        in_maps.append({
            "xT": np.ascontiguousarray(x[bb].T),
            "BreT": BreT, "BimT": BimT, "CreT": CreT, "CimT": CimT,
            "nul": nul, "thl": thl, "Dv": Dv, "nw": nw,
            "wgvL": wgvL, "woutT": woutT,
        })
    res = run_bass_kernel_spmd(nc, in_maps, core_ids=list(range(BATCH)))
    out = np.empty((BATCH, T, D), dtype=np.float32)
    for bb in range(BATCH):
        out[bb] = np.asarray(res.results[bb]["outT"]).T
    return out


# version 16 (final)
# speedup vs baseline: 1.0492x; 1.0492x over previous
import numpy as np
import ml_dtypes
import concourse.bass as bass
import concourse.mybir as mybir
from concourse.bass_utils import run_bass_kernel_spmd

F32 = mybir.dt.float32
BF16 = mybir.dt.bfloat16
I32 = mybir.dt.int32
AF = mybir.ActivationFunctionType
OP = mybir.AluOpType
AX = mybir.AxisListType

D = 2048
T = 2048
NS = 256
OH = 8192
DT = D // 128
PAIRS = OH // 128
NQ = 4
EPS = 1e-5
QB = 127.0
TWO_PI = 6.283185307179586

ENGINES = ("sync", "gpsimd", "vector", "scalar", "tensor")
GR = 2048


def R(name, e0, e1, es=4):
    b0, b1 = e0 * es, e1 * es
    return [(name, i) for i in range(b0 // GR, -(-b1 // GR))]


class B:
    def __init__(self):
        self.stream = {e: [] for e in ENGINES}
        self.semnames = []
        self.cnt = {}
        self.waited = {e: {} for e in ENGINES}
        self.lastw = {}
        self.readers = {}
        self.pending = {e: [] for e in ENGINES}
        self.qrot = {}

    def sem(self, name):
        self.semnames.append(name)
        self.cnt[name] = 0

    def _wait(self, eng, sem, val):
        if val <= 0 or self.waited[eng].get(sem, 0) >= val:
            return
        self.waited[eng][sem] = val
        self.stream[eng].append(("w", sem, val))

    def _deps(self, eng, reads, writes):
        for r in reads:
            lw = self.lastw.get(r)
            if lw:
                self._wait(eng, lw[0], lw[1])
        for r in writes:
            lw = self.lastw.get(r)
            if lw:
                self._wait(eng, lw[0], lw[1])
            for rd in self.readers.get(r, ()):
                self._wait(eng, rd[0], rd[1])

    def op(self, eng, fn, reads=(), writes=(), inc=True):
        self._deps(eng, reads, writes)
        if inc:
            sem = "E" + eng
            self.cnt[sem] += 1
            v = self.cnt[sem]
            self.stream[eng].append(("o", fn, sem, 1))
            tok = (sem, v)
            for r in self.pending[eng]:
                self.readers.setdefault(r, []).append(tok)
            self.pending[eng] = []
            for r in reads:
                self.readers.setdefault(r, []).append(tok)
            for r in writes:
                self.lastw[r] = tok
                self.readers[r] = []
        else:
            self.stream[eng].append(("o", fn, None, 0))
            self.pending[eng].extend(reads)
            self.pending[eng].extend(writes)

    def dma(self, eng, fn, reads=(), writes=(), q=None):
        self._deps(eng, reads, writes)
        base = q or ("D" + eng)
        idx = self.qrot.get(base, 0)
        self.qrot[base] = (idx + 1) % 8
        sem = f"{base}.{idx}"
        self._wait(eng, sem, self.cnt[sem])
        self.cnt[sem] += 16
        v = self.cnt[sem]
        self.stream[eng].append(("d", fn, sem, 16))
        tok = (sem, v)
        for r in reads:
            self.readers.setdefault(r, []).append(tok)
        for r in writes:
            self.lastw[r] = tok
            self.readers[r] = []


def build_program(debug=False):
    nc = bass.Bass("TRN2", target_bir_lowering=False)
    b = B()
    for e in ENGINES:
        b.sem("E" + e)
    for q in ("Dsync", "Dgpsimd", "Dscalar", "Dout", "Dwq"):
        for i in range(8):
            b.sem(f"{q}.{i}")

    xT_h = nc.dram_tensor("xT", [D, T], F32, kind="ExternalInput").ap()
    BreT_h = nc.dram_tensor("BreT", [D, NS], F32, kind="ExternalInput").ap()
    BimT_h = nc.dram_tensor("BimT", [D, NS], F32, kind="ExternalInput").ap()
    CreT_h = nc.dram_tensor("CreT", [NS, D], F32, kind="ExternalInput").ap()
    CimT_h = nc.dram_tensor("CimT", [NS, D], F32, kind="ExternalInput").ap()
    nul_h = nc.dram_tensor("nul", [1, NS], F32, kind="ExternalInput").ap()
    thl_h = nc.dram_tensor("thl", [1, NS], F32, kind="ExternalInput").ap()
    Dv_h = nc.dram_tensor("Dv", [1, D], F32, kind="ExternalInput").ap()
    nw_h = nc.dram_tensor("nw", [1, D], F32, kind="ExternalInput").ap()
    wgvL_h = nc.dram_tensor("wgvL", [128, 128, DT, 128], F32,
                            kind="ExternalInput").ap()
    woutT_h = nc.dram_tensor("woutT", [OH, D], F32, kind="ExternalInput").ap()

    id_h = nc.inline_tensor(np.eye(128, dtype=np.float32), "ident").ap()
    idb_h = nc.inline_tensor(np.eye(128, dtype=ml_dtypes.bfloat16), "identb").ap()
    iota_h = nc.inline_tensor(np.arange(T, dtype=np.float32).reshape(1, T), "iota").ap()
    ones_h = nc.inline_tensor(np.ones((1, 128), np.float32), "onesr").ap()
    onesc_h = nc.inline_tensor(np.ones((128, 1), np.float32), "onescf").ap()

    ik = "ExternalOutput" if debug else "Internal"
    h2sp = nc.dram_tensor("h2sp", [OH, T], BF16, kind=ik).ap()
    x1sp = nc.dram_tensor("x1sp", [D, T], F32, kind=ik).ap()
    z2sp = nc.dram_tensor("z2sp", [D, T], F32, kind=ik).ap()
    wq2sp = nc.dram_tensor("wq2sp", [8, 8, 128, 8, 256], BF16, kind=ik).ap()
    if debug:
        tapBu = nc.dram_tensor("tapBu", [2, 128, 4096], F32, kind="ExternalOutput").ap()
        tapH = nc.dram_tensor("tapH", [2, 128, 4096], BF16, kind="ExternalOutput").ap()
        tapDQ = nc.dram_tensor("tapDQ", [128, T], F32, kind="ExternalOutput").ap()
        tapAM = nc.dram_tensor("tapAM", [1, T], F32, kind="ExternalOutput").ap()
        tapA2 = nc.dram_tensor("tapA2", [128, DT * T], mybir.dt.bfloat16,
                               kind="ExternalOutput").ap()
    outT_h = nc.dram_tensor("outT", [D, T], F32, kind="ExternalOutput").ap()

    h2sp_t = h2sp.rearrange("(j p) t -> p j t", p=128)
    x1sp_t = x1sp.rearrange("(a p) t -> p a t", p=128)
    z2sp_t = z2sp.rearrange("(a p) t -> p a t", p=128)
    out_t = outT_h.rearrange("(a p) t -> p a t", p=128)
    wout_t = woutT_h.rearrange("(j p) d -> p j d", p=128)

    sb = lambda n, s, d=F32: nc.alloc_sbuf_tensor(n, s, d)
    c_id = sb("c_id", [128, 128])
    c_idb = sb("c_idb", [128, 128], BF16)
    c_ones = sb("c_ones", [1, 128])
    c_onesc = sb("c_onesc", [128, 1])
    rows = sb("rows", [1, 1792])
    srow = sb("srow", [1, 8])
    colD = sb("colD", [128, DT])
    colNW = sb("colNW", [128, DT])
    colG = sb("colG", [128, 2])
    colM = sb("colM", [128, 2])
    colS = sb("colS", [128, 2])
    colS2 = sb("colS2", [128, 2])
    colSG = sb("colSG", [128, 1])
    slots = sb("slots", [128, 192])

    P1 = sb("P1", [128, DT * T], BF16)
    P2 = sb("P2", [128, 8192])
    P3 = sb("P3", [128, 4096])
    HBF = sb("HBF", [128, 2 * T], BF16)
    HBF2 = sb("HBF2", [128, 2 * T], BF16)
    CBF = sb("CBF", [128, 2 * D], BF16)
    CBF2 = sb("CBF2", [128, 2 * D], BF16)
    WOF = sb("WOF", [128, 2048])
    WR = sb("WR", [128, 4096], BF16)
    CH_HS = sb("CH_HS", [128, 512])
    CH_I = sb("CH_I", [128, 512], I32)
    CH_CF = sb("CH_CF", [128, 512])
    MXA = sb("MXA", [128, 2048])
    MNA = sb("MNA", [128, 2048])
    Z2S = sb("Z2S", [128, 2048])
    OTS = sb("OTS", [128, 1024])
    REP = sb("REP", [128, 1024])

    WV = P2
    BBF = P2[:, 0:4096].bitcast(BF16)
    dqa1rep = P3[:, 0:2048]
    WOQ = P3[:, 2048:4096].bitcast(BF16)
    r_amax2 = P3[0:1, 0:2048]
    SG1 = HBF[:, 0:2048]
    SG2 = HBF[:, 2048:4096]
    GS = HBF2[:, 0:2048].bitcast(F32)
    SIL = HBF2[:, 2048:4096].bitcast(F32)
    T2 = CBF[:, 0:2048].bitcast(F32)
    H2T = CBF[:, 2048:4096]
    H2Q = CBF[:, 2048:3072]
    X1S = CBF2[:, 0:2048].bitcast(F32)
    MXB = MXA[:, 0:1024].bitcast(BF16)
    MNB = MXA[:, 1024:2048].bitcast(BF16)
    SREP = MXA[:, 1536:2048]
    srep_row = MXA[0:1, 1536:2048]
    c_iota = Z2S[0:1, 0:2048]

    def WQap(k):
        if k < 2:
            return CBF2[:, k * 2048:(k + 1) * 2048]
        return MNA[:, (k - 2) * 1024:(k - 1) * 1024].bitcast(BF16)

    def WQreg(k):
        if k < 2:
            return R("CBF2", k * 2048, (k + 1) * 2048, es=2)
        return R("MNA", (k - 2) * 1024, (k - 1) * 1024, es=4)

    psA = nc.alloc_psum_tensor("psA", [128, 2048], F32)
    psB = nc.alloc_psum_tensor("psB", [128, 2048], F32)
    PSA = lambda k: psA[:, k * 512:(k + 1) * 512]
    PSB = lambda k: psB[:, k * 512:(k + 1) * 512]
    RA = lambda k: ("psA", k)
    RB = lambda k: ("psB", k)

    def bcast_row(row_ap, rreads, psk, pskreg, out_ap, outwrites, scale=None):
        b.op("tensor", lambda t: t.matmul(psk, c_ones[:, 0:128], row_ap,
                                          start=True, stop=True),
             reads=list(rreads) + [("c_ones", 0)], writes=[pskreg])
        if scale is None:
            b.op("vector", lambda v: v.tensor_copy(out_ap, psk),
                 reads=[pskreg], writes=list(outwrites))
        else:
            b.op("vector", lambda v: v.tensor_scalar(out_ap, psk, scale, None,
                                                     OP.mult),
                 reads=[pskreg], writes=list(outwrites))

    RN = [
        (CH_HS[:, 0:512], [("CH_HS", 0)],
         CH_I[:, 0:512], [("CH_I", 0)], CH_CF[:, 0:512], [("CH_CF", 0)]),
        (MNA[:, 1024:1536], R("MNA", 1024, 1536),
         MNA[:, 0:512].bitcast(I32), R("MNA", 0, 512),
         MNA[:, 512:1024], R("MNA", 512, 1024)),
    ]

    def round_neg(k, dst_ap, dst_regs):
        src_ap, src_regs, i_ap, i_regs, cf_ap, cf_regs = RN[k]
        b.op("vector", lambda v: v.tensor_copy(i_ap, src_ap),
             reads=src_regs, writes=i_regs)
        b.op("vector", lambda v: v.tensor_copy(cf_ap, i_ap),
             reads=i_regs, writes=cf_regs)
        b.op("vector", lambda v: v.tensor_tensor(src_ap, src_ap,
                                                 cf_ap, OP.subtract),
             reads=src_regs + cf_regs, writes=src_regs)
        b.op("vector", lambda v: v.scalar_tensor_tensor(
            cf_ap, src_ap, 0.5, cf_ap, OP.is_ge, OP.add),
            reads=src_regs + cf_regs, writes=cf_regs)
        b.op("vector", lambda v: v.scalar_tensor_tensor(
            dst_ap, src_ap, -0.5, cf_ap, OP.is_le, OP.subtract),
            reads=src_regs + cf_regs, writes=dst_regs)

    def rs(i):
        return rows[0:1, i * 512:(i + 1) * 512]

    def pa(i):
        return rows[0:1, i * 256:(i + 1) * 256]

    def repr_(s):
        return REP[0:1, s * 512:(s + 1) * 512]

    theta_row = rows[0:1, 1536:1792]
    RTH = [("rows", 3)]

    b.op("vector", lambda v: v.memset(srow[0:1, 4:5], EPS), writes=[("srow", 0)])

    for part in range(4):
        lo = part * 4
        b.dma("gpsimd", lambda g, lo=lo: g.dma_start(
            out=P1.ap().rearrange("p (a t) -> p a t", a=DT)[:, lo:lo + 4, :],
            in_=xT_h.rearrange("(a p) t -> p a t", p=128)[:, lo:lo + 4, :]),
            writes=R("P1", lo * T, (lo + 4) * T, es=2))

    b.dma("sync", lambda s: s.dma_start(out=c_id[:], in_=id_h[:]), writes=[("c_id", 0)])
    b.dma("sync", lambda s: s.dma_start(out=c_idb[:], in_=idb_h[:]), writes=[("c_idb", 0)])
    b.dma("sync", lambda s: s.dma_start(out=c_ones[:], in_=ones_h[:]), writes=[("c_ones", 0)])
    b.dma("sync", lambda s: s.dma_start(out=c_onesc[:], in_=onesc_h[:]), writes=[("c_onesc", 0)])
    b.dma("sync", lambda s: s.dma_start(out=pa(0), in_=nul_h[:]), writes=R("rows", 0, 256))
    b.dma("sync", lambda s: s.dma_start(out=pa(6), in_=thl_h[:]), writes=RTH)

    b.op("scalar", lambda s: s.activation(pa(1), pa(0), AF.Abs),
         reads=R("rows", 0, 256), writes=R("rows", 256, 512))
    b.op("scalar", lambda s: s.activation(pa(2), pa(1), AF.Exp, scale=-1.0),
         reads=R("rows", 256, 512), writes=R("rows", 512, 768))
    b.op("vector", lambda v: v.tensor_scalar(pa(2), pa(2), 1.0, None, OP.add),
         reads=R("rows", 512, 768), writes=R("rows", 512, 768))
    b.op("scalar", lambda s: s.activation(pa(2), pa(2), AF.Ln),
         reads=R("rows", 512, 768), writes=R("rows", 512, 768))
    b.op("vector", lambda v: v.tensor_scalar(pa(1), pa(0), 0.0, None, OP.max),
         reads=R("rows", 0, 256), writes=R("rows", 256, 512))
    b.op("vector", lambda v: v.tensor_tensor(pa(1), pa(1), pa(2), OP.add),
         reads=R("rows", 256, 768), writes=R("rows", 256, 512))
    b.op("scalar", lambda s: s.activation(pa(2), pa(1), AF.Exp),
         reads=R("rows", 256, 512), writes=R("rows", 512, 768))
    b.op("scalar", lambda s: s.activation(pa(2), pa(2), AF.Exp, scale=-1.0),
         reads=R("rows", 512, 768), writes=R("rows", 512, 768))
    b.op("scalar", lambda s: s.activation(theta_row, pa(6), AF.Exp),
         reads=RTH, writes=RTH)
    b.op("scalar", lambda s: s.activation(pa(3), pa(2), AF.Square),
         reads=R("rows", 512, 768), writes=R("rows", 768, 1024))
    b.op("vector", lambda v: v.tensor_scalar(pa(3), pa(3), -1.0, 1.0, OP.mult, OP.add),
         reads=R("rows", 768, 1024), writes=R("rows", 768, 1024))
    b.op("vector", lambda v: v.tensor_scalar(pa(3), pa(3), 0.0, None, OP.max),
         reads=R("rows", 768, 1024), writes=R("rows", 768, 1024))
    b.op("scalar", lambda s: s.activation(pa(3), pa(3), AF.Sqrt),
         reads=R("rows", 768, 1024), writes=R("rows", 768, 1024))

    for nh in range(2):
        b.op("tensor", lambda t, nh=nh: t.transpose(
            PSB(3)[:, nh:nh + 1], rows[0:1, 512 + nh * 128:512 + (nh + 1) * 128],
            c_id[0:1, 0:1]),
            reads=R("rows", 512, 768) + [("c_id", 0)], writes=[RB(3)])
        b.op("tensor", lambda t, nh=nh: t.transpose(
            PSB(3)[:, 2 + nh:3 + nh], rows[0:1, 768 + nh * 128:768 + (nh + 1) * 128],
            c_id[0:1, 0:1]),
            reads=R("rows", 768, 1024) + [("c_id", 0)], writes=[RB(3)])
    b.op("vector", lambda v: v.tensor_copy(colM[:], PSB(3)[:, 0:2]),
         reads=[RB(3)], writes=[("colM", 0)])
    b.op("vector", lambda v: v.tensor_copy(colG[:], PSB(3)[:, 2:4]),
         reads=[RB(3)], writes=[("colG", 0)])

    for which, src_h, coltgt, colreg in ((0, Dv_h, colD, ("colD", 0)),
                                         (1, nw_h, colNW, ("colNW", 0))):
        for k in range(4):
            b.dma("sync", lambda s, src_h=src_h, k=k: s.dma_start(
                out=rs(2), in_=src_h[0:1, k * 512:(k + 1) * 512]),
                writes=R("rows", 1024, 1536))
            for j in range(4):
                b.op("tensor", lambda t, k=k, j=j: t.transpose(
                    PSB(2)[:, 4 * k + j:4 * k + j + 1],
                    rows[0:1, 1024 + j * 128:1024 + (j + 1) * 128], c_id[0:1, 0:1]),
                    reads=R("rows", 1024, 1536) + [("c_id", 0)], writes=[RB(2)])
        if which == 0:
            b.op("vector", lambda v, coltgt=coltgt: v.tensor_scalar(
                coltgt[:], PSB(2)[:, 0:DT], 1.0, None, OP.add),
                reads=[RB(2)], writes=[colreg])
        else:
            b.op("vector", lambda v, coltgt=coltgt: v.tensor_copy(
                coltgt[:], PSB(2)[:, 0:DT]),
                reads=[RB(2)], writes=[colreg])

    b.dma("sync", lambda s: s.dma_start(
        out=P2[:, 4096:8192].rearrange("p (a n) -> p a n", a=DT),
        in_=BreT_h.rearrange("(a p) n -> p a n", p=128)),
        writes=R("P2", 4096, 8192))
    b.op("vector", lambda v: v.tensor_copy(BBF[:, 0:4096], P2[:, 4096:8192]),
         reads=R("P2", 4096, 8192), writes=R("P2", 0, 2048))
    b.dma("sync", lambda s: s.dma_start(
        out=P2[:, 4096:8192].rearrange("p (a n) -> p a n", a=DT),
        in_=BimT_h.rearrange("(a p) n -> p a n", p=128)),
        writes=R("P2", 4096, 8192))
    b.op("vector", lambda v: v.tensor_copy(BBF[:, 4096:8192], P2[:, 4096:8192]),
         reads=R("P2", 4096, 8192), writes=R("P2", 2048, 4096))

    for nh in range(2):
        CT = CBF if nh == 0 else CBF2
        cn = "CBF" if nh == 0 else "CBF2"
        b.dma("sync", lambda s, nh=nh: s.dma_start(
            out=Z2S[:], in_=CreT_h[nh * 128:(nh + 1) * 128, :]),
            writes=R("Z2S", 0, 2048))
        b.op("vector", lambda v, CT=CT: v.tensor_copy(CT[:, 0:2048], Z2S[:]),
             reads=R("Z2S", 0, 2048), writes=R(cn, 0, 2048, es=2))
        b.dma("sync", lambda s, nh=nh: s.dma_start(
            out=Z2S[:], in_=CimT_h[nh * 128:(nh + 1) * 128, :]),
            writes=R("Z2S", 0, 2048))
        b.op("vector", lambda v, CT=CT: v.tensor_scalar(
            CT[:, 2048:4096], Z2S[:], -1.0, None, OP.mult),
            reads=R("Z2S", 0, 2048), writes=R(cn, 2048, 4096, es=2))
    b.dma("sync", lambda s: s.dma_start(out=c_iota, in_=iota_h[:]),
          writes=R("Z2S", 0, 2048))

    wout4 = woutT_h.rearrange("(r p) f -> r p f", p=128)
    NGV = 128
    NWO = wout4.shape[0]

    def wgv_chunk(ci):
        return wgvL_h[ci].rearrange("p a o -> p (a o)")

    def wout_chunk(ci):
        return wout4[ci]

    WRF = WR[:, 0:4096].bitcast(F32)

    def emit_abs_chunk(src_ap, slot):
        if slot % 2 == 0:
            buf, breg = WOF[:], R("WOF", 0, 2048)
        else:
            buf, breg = WRF[:, 0:2048], R("WR", 0, 2048)
        b.dma("scalar", lambda s, src_ap=src_ap, buf=buf: s.dma_start(
            out=buf, in_=src_ap), writes=breg, q="Dscalar")
        b.op("scalar", lambda s, slot=slot, buf=buf: s.activation(
            buf, buf, AF.Abs, accum_out=slots[:, slot:slot + 1]),
            reads=breg, writes=breg + [("slots", slot)])

    for ci in range(24):
        emit_abs_chunk(wgv_chunk(ci), ci)

    bre_ap = P2[:, 4096:6144]
    bim_ap = P2[:, 6144:8192]
    BRE = R("P2", 4096, 6144)
    BIM = R("P2", 6144, 8192)
    za, zb = P3[:, 0:2048], P3[:, 2048:4096]
    ZA = R("P3", 0, 2048)
    ZB = R("P3", 2048, 4096)
    cos_ap, sin_ap = MXA[:], MNA[:]
    COS = R("MXA", 0, 2048)
    SIN = R("MNA", 0, 2048)

    TRS = [
        (CH_HS[:], [("CH_HS", 0)], CH_I[:], [("CH_I", 0)],
         CH_CF[:], [("CH_CF", 0)], PSB(0), RB(0)),
        (OTS[:, 0:512], [("OTS", 0)], OTS[:, 512:1024].bitcast(I32), [("OTS", 1)],
         REP[:, 0:512], [("REP", 0)], PSB(1), RB(1)),
    ]
    ti = 0
    for nh in range(2):
        for which, dst, dregs in ((0, cos_ap, COS), (1, sin_ap, SIN)):
            for q in range(NQ):
                u_ap, u_r, i_ap, i_r, cf_ap, cf_r, ps, psr = TRS[ti % 2]
                ti += 1
                b.op("tensor", lambda t, nh=nh, q=q, ps=ps: t.matmul(
                    ps, theta_row[0:1, nh * 128:(nh + 1) * 128],
                    c_iota[0:1, q * 512:(q + 1) * 512], start=True, stop=True),
                    reads=RTH + R("Z2S", 0, 2048), writes=[psr])
                off = 0.25 if which == 0 else 0.0
                b.op("vector", lambda v, off=off, u_ap=u_ap, ps=ps: v.tensor_scalar(
                    u_ap, ps, 1.0 / TWO_PI, off, OP.mult, OP.add),
                    reads=[psr], writes=u_r)
                b.op("vector", lambda v, i_ap=i_ap, u_ap=u_ap: v.tensor_copy(
                    i_ap, u_ap), reads=u_r, writes=i_r)
                b.op("gpsimd", lambda g, cf_ap=cf_ap, i_ap=i_ap: g.tensor_copy(
                    cf_ap, i_ap), reads=i_r, writes=cf_r)
                b.op("vector", lambda v, u_ap=u_ap, cf_ap=cf_ap: v.tensor_tensor(
                    u_ap, u_ap, cf_ap, OP.subtract),
                    reads=u_r + cf_r, writes=u_r)
                b.op("vector", lambda v, u_ap=u_ap, cf_ap=cf_ap: v.scalar_tensor_tensor(
                    cf_ap, u_ap, 0.5, u_ap, OP.is_gt, OP.subtract),
                    reads=u_r, writes=cf_r)
                b.op("scalar", lambda s, dst=dst, q=q, cf_ap=cf_ap: s.activation(
                    dst[:, q * 512:(q + 1) * 512], cf_ap, AF.Sin, scale=-TWO_PI),
                    reads=cf_r, writes=[dregs[q]])
        for q in range(NQ):
            for comp in range(2):
                for a in range(DT):
                    b.op("tensor", lambda t, nh=nh, q=q, comp=comp, a=a: t.matmul(
                        PSA(comp),
                        BBF[:, comp * 4096 + a * 256 + nh * 128:
                            comp * 4096 + a * 256 + (nh + 1) * 128],
                        P1[:, a * T + q * 512:a * T + (q + 1) * 512],
                        start=(a == 0), stop=(a == DT - 1)),
                        reads=R("P2", comp * 2048 + a * 128, comp * 2048 + (a + 1) * 128) +
                              R("P1", a * T + q * 512, a * T + (q + 1) * 512, es=2),
                        writes=[RA(comp)], inc=(a == DT - 1))
            for comp, dst_ap, dregs in ((0, bre_ap, BRE), (1, bim_ap, BIM)):
                b.op("vector", lambda v, q=q, comp=comp, dst_ap=dst_ap, nh=nh:
                     v.tensor_scalar(dst_ap[:, q * 512:(q + 1) * 512], PSA(comp),
                                     colG[:, nh:nh + 1], None, OP.mult),
                     reads=[RA(comp), ("colG", 0)], writes=[dregs[q]])
        if debug:
            b.dma("sync", lambda s, nh=nh: s.dma_start(
                out=tapBu[nh][:, 0:2048], in_=bre_ap), reads=BRE, q="Dout")
            b.dma("sync", lambda s, nh=nh: s.dma_start(
                out=tapBu[nh][:, 2048:4096], in_=bim_ap), reads=BIM, q="Dout")
        b.op("vector", lambda v: v.tensor_tensor(za, bre_ap, cos_ap, OP.mult),
             reads=BRE + COS, writes=ZA)
        b.op("gpsimd", lambda g: g.tensor_tensor(zb, bim_ap, sin_ap, OP.mult),
             reads=BIM + SIN, writes=ZB)
        b.op("vector", lambda v: v.tensor_tensor(za, za, zb, OP.add),
             reads=ZA + ZB, writes=ZA)
        b.op("gpsimd", lambda g: g.tensor_tensor(zb, bim_ap, cos_ap, OP.mult),
             reads=BIM + COS, writes=ZB)
        b.op("vector", lambda v: v.tensor_tensor(bre_ap, bre_ap, sin_ap, OP.mult),
             reads=BRE + SIN, writes=BRE)
        b.op("vector", lambda v: v.tensor_tensor(zb, zb, bre_ap, OP.subtract),
             reads=ZB + BRE, writes=ZB)
        b.op("vector", lambda v, nh=nh: v.tensor_tensor_scan(
            bre_ap, colM[:, nh:nh + 1].broadcast_to([128, 2048]), za,
            0.0, OP.mult, OP.add),
            reads=ZA + [("colM", 0)], writes=BRE)
        b.op("vector", lambda v, nh=nh: v.tensor_tensor_scan(
            bim_ap, colM[:, nh:nh + 1].broadcast_to([128, 2048]), zb,
            0.0, OP.mult, OP.add),
            reads=ZB + [("colM", 0)], writes=BIM)
        b.op("vector", lambda v: v.tensor_tensor(za, bre_ap, cos_ap, OP.mult),
             reads=BRE + COS, writes=ZA)
        b.op("gpsimd", lambda g: g.tensor_tensor(zb, bim_ap, sin_ap, OP.mult),
             reads=BIM + SIN, writes=ZB)
        b.op("vector", lambda v: v.tensor_tensor(za, za, zb, OP.subtract),
             reads=ZA + ZB, writes=ZA)
        b.op("gpsimd", lambda g: g.tensor_tensor(zb, bim_ap, cos_ap, OP.mult),
             reads=BIM + COS, writes=ZB)
        b.op("vector", lambda v: v.tensor_tensor(bre_ap, bre_ap, sin_ap, OP.mult),
             reads=BRE + SIN, writes=BRE)
        b.op("vector", lambda v: v.tensor_tensor(zb, zb, bre_ap, OP.add),
             reads=ZB + BRE, writes=ZB)
        b.op("scalar", lambda s, nh=nh: s.copy(HBF[:, nh * T:(nh + 1) * T], za),
             reads=ZA, writes=R("HBF", nh * T, (nh + 1) * T, es=2))
        b.op("scalar", lambda s, nh=nh: s.copy(HBF2[:, nh * T:(nh + 1) * T], zb),
             reads=ZB, writes=R("HBF2", nh * T, (nh + 1) * T, es=2))
        if debug and nh == 1:
            b.dma("sync", lambda s: s.dma_start(out=tapH[0], in_=HBF[:]),
                  reads=R("HBF", 0, 4096, es=2), q="Dout")
            b.dma("sync", lambda s: s.dma_start(out=tapH[1], in_=HBF2[:]),
                  reads=R("HBF2", 0, 4096, es=2), q="Dout")
        for ci in range(24 + nh * 13, 24 + (nh + 1) * 13):
            emit_abs_chunk(wgv_chunk(ci), ci)

    MX1, MN1 = MXA[:, 0:512], MXA[:, 512:1024]
    RMX1, RMN1 = [("MXA", 0)], [("MXA", 1)]

    def emit_xload(q, lo=0, n=16):
        for part in range(lo // 4, (lo + n) // 4):
            pl = part * 4
            b.dma("sync", lambda s, q=q, pl=pl: s.dma_start(
                out=P2.ap().rearrange("p (a t) -> p a t", a=DT)[:, pl:pl + 4, :],
                in_=xT_h.rearrange("(a p) t -> p a t", p=128)[:, pl:pl + 4,
                                                              q * 512:(q + 1) * 512]),
                writes=R("P2", pl * 512, (pl + 4) * 512))

    def emit_mix_a(q, a):
        bank = a % 2
        for im in range(2):
            for nh in range(2):
                CT = CBF if nh == 0 else CBF2
                cn = "CBF" if nh == 0 else "CBF2"
                HT = HBF if im == 0 else HBF2
                hn = "HBF" if im == 0 else "HBF2"
                b.op("tensor", lambda t, q=q, a=a, im=im, nh=nh, bank=bank,
                     CT=CT, HT=HT: t.matmul(
                    PSA(bank), CT[:, im * D + a * 128:im * D + (a + 1) * 128],
                    HT[:, nh * T + q * 512:nh * T + (q + 1) * 512],
                    start=(im == 0 and nh == 0), stop=(im == 1 and nh == 1)),
                    reads=R(cn, im * D + a * 128, im * D + (a + 1) * 128, es=2) +
                          R(hn, nh * T + q * 512, nh * T + (q + 1) * 512, es=2),
                    writes=[RA(bank)], inc=(im == 1 and nh == 1))
        b.op("vector", lambda v, a=a, bank=bank: v.scalar_tensor_tensor(
            P2[:, a * 512:(a + 1) * 512], P2[:, a * 512:(a + 1) * 512],
            colD[:, a:a + 1], PSA(bank), OP.mult, OP.add),
            reads=R("P2", a * 512, (a + 1) * 512) + [("colD", 0), RA(bank)],
            writes=R("P2", a * 512, (a + 1) * 512))
        b.op("tensor", lambda t, a=a: t.matmul(
            psB[0:1, 1024:1536], c_onesc[:], P2[:, a * 512:(a + 1) * 512],
            start=(a == 0), stop=(a == DT - 1)),
            reads=R("P2", a * 512, (a + 1) * 512) + [("c_onesc", 0)],
            writes=[RB(2)] if a in (0, DT - 1) else [], inc=True)
        sq = a % 2
        b.op("scalar", lambda s, a=a, sq=sq: s.activation(
            Z2S[:, sq * 512:(sq + 1) * 512], P2[:, a * 512:(a + 1) * 512],
            AF.Square),
            reads=R("P2", a * 512, (a + 1) * 512),
            writes=R("Z2S", sq * 512, (sq + 1) * 512))
        b.op("tensor", lambda t, a=a, sq=sq: t.matmul(
            psB[0:1, 1536:2048], c_onesc[:], Z2S[:, sq * 512:(sq + 1) * 512],
            start=(a == 0), stop=(a == DT - 1)),
            reads=R("Z2S", sq * 512, (sq + 1) * 512) + [("c_onesc", 0)],
            writes=[RB(3)] if a in (0, DT - 1) else [], inc=True)

    emit_xload(0)
    for a in range(DT):
        emit_mix_a(0, a)
    for q in range(NQ):
        b.op("vector", lambda v: v.tensor_scalar(
            repr_(0), psB[0:1, 1024:1536], 1.0 / D, None, OP.mult),
            reads=[RB(2)], writes=[("REP", 0)])
        b.op("vector", lambda v: v.tensor_scalar(
            rs(1), psB[0:1, 1536:2048], 1.0 / D, None, OP.mult),
            reads=[RB(3)], writes=R("rows", 512, 1024))
        b.op("vector", lambda v: v.tensor_tensor(rs(0), repr_(0), repr_(0), OP.mult),
             reads=[("REP", 0)], writes=R("rows", 0, 512))
        b.op("vector", lambda v: v.tensor_tensor(rs(1), rs(1), rs(0), OP.subtract),
             reads=R("rows", 0, 1024), writes=R("rows", 512, 1024))
        b.op("scalar", lambda s: s.activation(repr_(1), rs(1), AF.Sqrt,
                                              bias=srow[0:1, 4:5]),
             reads=R("rows", 512, 1024) + [("srow", 0)], writes=[("REP", 1)])
        b.op("vector", lambda v: v.reciprocal(repr_(1), repr_(1)),
             reads=[("REP", 1)], writes=[("REP", 1)])
        b.op("vector", lambda v: v.tensor_tensor(rs(2), repr_(1), repr_(1), OP.mult),
             reads=[("REP", 1)], writes=R("rows", 1024, 1536))
        b.op("vector", lambda v: v.tensor_tensor(rs(2), rs(2), rs(1), OP.mult),
             reads=R("rows", 512, 1536), writes=R("rows", 1024, 1536))
        b.op("scalar", lambda s: s.activation(rs(2), rs(2), AF.Sqrt,
                                              bias=srow[0:1, 4:5]),
             reads=R("rows", 1024, 1536) + [("srow", 0)], writes=R("rows", 1024, 1536))
        b.op("vector", lambda v: v.reciprocal(rs(2), rs(2)),
             reads=R("rows", 1024, 1536), writes=R("rows", 1024, 1536))
        bcast_row(repr_(0), [("REP", 0)], PSA(2), RA(2), REP[:, 0:512], [("REP", 0)])
        bcast_row(repr_(1), [("REP", 1)], PSA(3), RA(3), REP[:, 512:1024],
                  [("REP", 1)])
        for a in range(DT):
            k = a % 2
            b.op("vector", lambda v, a=a, k=k: v.tensor_tensor(
                OTS[:, k * 512:(k + 1) * 512], P2[:, a * 512:(a + 1) * 512],
                REP[:, 0:512], OP.subtract),
                reads=R("P2", a * 512, (a + 1) * 512) + [("REP", 0)],
                writes=[("OTS", k)])
            b.op("vector", lambda v, k=k: v.tensor_tensor(
                OTS[:, k * 512:(k + 1) * 512], OTS[:, k * 512:(k + 1) * 512],
                REP[:, 512:1024], OP.mult),
                reads=[("OTS", k), ("REP", 1)], writes=[("OTS", k)])
            b.dma("gpsimd", lambda g, q=q, a=a, k=k: g.dma_start(
                out=x1sp_t[:, a:a + 1, q * 512:(q + 1) * 512],
                in_=OTS.ap().rearrange("p (s t) -> p s t", s=2)[:, k:k + 1, :]),
                reads=[("OTS", k)], writes=[("x1sp", a, q)])
            if a == 0:
                b.op("vector", lambda v, a=a, k=k: v.tensor_scalar(
                    MX1, OTS[:, k * 512:(k + 1) * 512], colNW[:, a:a + 1], None,
                    OP.mult),
                    reads=[("OTS", k), ("colNW", 0)], writes=RMX1)
                b.op("vector", lambda v, a=a, k=k: v.tensor_scalar(
                    MN1, OTS[:, k * 512:(k + 1) * 512], colNW[:, a:a + 1], None,
                    OP.mult),
                    reads=[("OTS", k), ("colNW", 0)], writes=RMN1)
            else:
                b.op("vector", lambda v, a=a, k=k: v.scalar_tensor_tensor(
                    MX1, OTS[:, k * 512:(k + 1) * 512], colNW[:, a:a + 1], MX1,
                    OP.mult, OP.max),
                    reads=[("OTS", k), ("colNW", 0)] + RMX1, writes=RMX1)
                b.op("vector", lambda v, a=a, k=k: v.scalar_tensor_tensor(
                    MN1, OTS[:, k * 512:(k + 1) * 512], colNW[:, a:a + 1], MN1,
                    OP.mult, OP.min),
                    reads=[("OTS", k), ("colNW", 0)] + RMN1, writes=RMN1)
            if q + 1 < NQ:
                if a % 4 == 3:
                    emit_xload(q + 1, a - 3, 4)
                if a >= 4:
                    emit_mix_a(q + 1, a - 4)
        b.op("vector", lambda v: v.tensor_scalar(MN1, MN1, -1.0, None, OP.mult),
             reads=RMN1, writes=RMN1)
        b.op("vector", lambda v: v.tensor_tensor(MX1, MX1, MN1, OP.max),
             reads=RMX1 + RMN1, writes=RMX1)
        for j in range(4):
            b.op("tensor", lambda t, j=j: t.transpose(
                PSB(1)[:, 0:128], MXA[:, j * 128:(j + 1) * 128], c_id[:]),
                reads=RMX1 + [("c_id", 0)], writes=[RB(1)])
            b.op("vector", lambda v, j=j: v.reduce_max(
                MXA[:, 1024 + j:1025 + j], PSB(1)[:, 0:128], axis=AX.X),
                reads=[RB(1)], writes=[("MXA", 2)])
        b.op("tensor", lambda t: t.transpose(
            PSB(1)[0:4, 0:128], MXA[:, 1024:1028], c_id[:]),
            reads=[("MXA", 2), ("c_id", 0)], writes=[RB(1)])
        b.op("vector", lambda v: v.tensor_copy(MXA[0:4, 1056:1184],
                                               PSB(1)[0:4, 0:128]),
             reads=[RB(1)], writes=[("MXA", 2)])
        b.dma("gpsimd", lambda g: g.dma_start(
            out=rs(0).rearrange("e (j p) -> e j p", j=4),
            in_=MXA[0:4, 1056:1184]),
            reads=[("MXA", 2)], writes=R("rows", 0, 512))
        b.op("vector", lambda v: v.tensor_tensor(rs(0), rs(0), rs(2), OP.mult),
             reads=R("rows", 0, 512) + R("rows", 1024, 1536),
             writes=R("rows", 0, 512))
        b.op("vector", lambda v: v.tensor_scalar(rs(0), rs(0), 1e-5, None, OP.max),
             reads=R("rows", 0, 512), writes=R("rows", 0, 512))
        b.op("vector", lambda v: v.reciprocal(rs(1), rs(0)),
             reads=R("rows", 0, 512), writes=R("rows", 512, 1024))
        b.op("vector", lambda v: v.tensor_scalar(rs(1), rs(1), QB, None, OP.mult),
             reads=R("rows", 512, 1024), writes=R("rows", 512, 1024))
        bcast_row(rs(0), R("rows", 0, 512), PSA(3), RA(3),
                  dqa1rep[:, q * 512:(q + 1) * 512],
                  R("P3", q * 512, (q + 1) * 512), scale=1.0 / QB)
        b.op("vector", lambda v: v.tensor_tensor(rs(2), rs(2), rs(1), OP.mult),
             reads=R("rows", 512, 1536), writes=R("rows", 1024, 1536))
        bcast_row(rs(2), R("rows", 1024, 1536), PSA(2), RA(2),
                  SREP, [("MXA", 3)])
        for a in range(DT):
            k = a % 2
            b.dma("sync", lambda s, q=q, a=a, k=k: s.dma_start(
                out=OTS.ap().rearrange("p (s t) -> p s t", s=2)[:, k:k + 1, :],
                in_=x1sp_t[:, a:a + 1, q * 512:(q + 1) * 512]),
                reads=[("x1sp", a, q)], writes=[("OTS", k)])
            src_ap, src_regs = RN[k][0], RN[k][1]
            b.op("vector", lambda v, a=a, k=k, src_ap=src_ap: v.scalar_tensor_tensor(
                src_ap, OTS[:, k * 512:(k + 1) * 512], colNW[:, a:a + 1], SREP,
                OP.mult, OP.mult),
                reads=[("OTS", k), ("colNW", 0), ("MXA", 3)], writes=src_regs)
            round_neg(k, P1[:, a * T + q * 512:a * T + (q + 1) * 512],
                      R("P1", a * T + q * 512, a * T + (q + 1) * 512, es=2))
            if q + 1 < NQ and a < 4:
                emit_mix_a(q + 1, 12 + a)
        for ci in range(50 + q * 20, min(NGV, 70 + q * 20)):
            emit_abs_chunk(wgv_chunk(ci), ci)

    b.op("vector", lambda v: v.reduce_sum(
        MXA[0:128, 0:1], slots[:, 0:128], axis=AX.X),
        reads=[("slots", i) for i in range(NGV)], writes=[("MXA", 0)])
    b.op("tensor", lambda t: t.matmul(
        psB[0:1, 1024:1025], c_onesc[:], MXA[0:128, 0:1], start=True, stop=True),
        reads=[("MXA", 0), ("c_onesc", 0)], writes=[RB(2)])
    b.op("vector", lambda v: v.tensor_scalar(
        srow[0:1, 0:1], psB[0:1, 1024:1025], 1.0 / (D * 2.0 * OH), None, OP.mult),
        reads=[RB(2)], writes=[("srow", 1)])
    b.op("vector", lambda v: v.tensor_scalar(
        srow[0:1, 0:1], srow[0:1, 0:1], 1e-5, None, OP.max),
        reads=[("srow", 1)], writes=[("srow", 1)])
    b.op("tensor", lambda t: t.matmul(
        psB[:, 1024:1025], c_ones[:, 0:128], srow[0:1, 0:1], start=True, stop=True),
        reads=[("srow", 1), ("c_ones", 0)], writes=[RB(2)])
    b.op("vector", lambda v: v.tensor_scalar(
        colS[:, 0:1], psB[:, 1024:1025], 0.5, None, OP.mult),
        reads=[RB(2)], writes=[("colS", 0)])
    b.op("vector", lambda v: v.tensor_scalar(
        colS[:, 1:2], psB[:, 1024:1025], -0.5, None, OP.mult),
        reads=[RB(2)], writes=[("colS", 0)])
    b.op("vector", lambda v: v.tensor_scalar(
        colSG[:, 0:1], psB[:, 1024:1025], -0.5, None, OP.mult),
        reads=[RB(2)], writes=[("colSG", 0)])
    b.op("vector", lambda v: v.tensor_scalar(
        dqa1rep[:], dqa1rep[:], colSG[:, 0:1], None, OP.mult),
        reads=R("P3", 0, 2048) + [("colSG", 0)], writes=R("P3", 0, 2048))
    if debug:
        b.dma("sync", lambda s: s.dma_start(out=tapDQ[:], in_=dqa1rep[:]),
              reads=R("P3", 0, 2048), q="Dout")

    def wload(p, side, sl):
        blk = side * 64 + p
        b.dma("sync", lambda s, blk=blk, sl=sl: s.dma_start(
            out=WV[:, sl * 2048:(sl + 1) * 2048].rearrange("p (a o) -> p a o", a=DT),
            in_=wgvL_h[blk]),
            writes=R("P2", sl * 2048, (sl + 1) * 2048))

    def wquant(sl, alt):
        SGa = HBF[:, alt * 2048:(alt + 1) * 2048]
        SGar = R("HBF", alt * 2048, (alt + 1) * 2048, es=2)
        b.op("scalar", lambda s, sl=sl: s.activation(
            WQap(sl), WV[:, sl * 2048:(sl + 1) * 2048], AF.Sign, bias=colS[:, 1:2]),
            reads=R("P2", sl * 2048, (sl + 1) * 2048) + [("colS", 0)],
            writes=WQreg(sl))
        b.op("scalar", lambda s, sl=sl, SGa=SGa: s.activation(
            SGa, WV[:, sl * 2048:(sl + 1) * 2048], AF.Sign, bias=colS[:, 0:1]),
            reads=R("P2", sl * 2048, (sl + 1) * 2048) + [("colS", 0)],
            writes=SGar)
        b.op("vector", lambda v, sl=sl, SGa=SGa: v.tensor_tensor(
            WQap(sl), WQap(sl), SGa, OP.add),
            reads=WQreg(sl) + SGar, writes=WQreg(sl))

    def emit_wout_prep(j):
        b.dma("scalar", lambda s, j=j: s.dma_start(out=WOF[:], in_=wout_t[:, j, :]),
              writes=R("WOF", 0, 2048), q="Dscalar")
        b.op("scalar", lambda s: s.activation(WOQ[:, 0:2048], WOF[:], AF.Sign,
                                              bias=colS2[:, 1:2]),
             reads=R("WOF", 0, 2048) + [("colS2", 0)], writes=R("P3", 2048, 3072))
        b.op("scalar", lambda s: s.activation(WR[:, 2048:4096], WOF[:], AF.Sign,
                                              bias=colS2[:, 0:1]),
             reads=R("WOF", 0, 2048) + [("colS2", 0)],
             writes=R("WR", 2048, 4096, es=2))
        b.op("vector", lambda v: v.tensor_tensor(
            WOQ[:, 0:2048], WOQ[:, 0:2048], WR[:, 2048:4096], OP.add),
            reads=R("P3", 2048, 3072) + R("WR", 2048, 4096, es=2),
            writes=R("P3", 2048, 3072))
        b.dma("gpsimd", lambda g, j=j: g.dma_start(
            out=wq2sp[:, j // 8, :, j % 8, :].rearrange("g p c -> p g c"),
            in_=WOQ[:, 0:2048].rearrange("p (g c) -> p g c", g=8)),
            reads=R("P3", 2048, 3072), writes=[("wq2sp", j)])

    wload(0, 0, 0)
    wload(0, 1, 1)
    wquant(0, 0)
    wquant(1, 1)
    woj = [0]

    for p in range(PAIRS):
        sv = (p % 2) * 2
        so = ((p + 1) % 2) * 2
        if p + 1 < PAIRS:
            wload(p + 1, 0, so)
            wload(p + 1, 1, so + 1)
        for th in range(2):
            if th == 1 and p + 1 < PAIRS:
                wquant(so, 0)
                wquant(so + 1, 1)
            PS = psA if th == 0 else psB
            RR = RA if th == 0 else RB
            for vg in range(2):
                for a in range(DT):
                    for tq in range(2):
                        last = (a == DT - 1)
                        b.op("tensor", lambda t, PS=PS, vg=vg, a=a, tq=tq, th=th,
                             sv=sv, last=last: t.matmul(
                            PS[:, (vg * 2 + tq) * 512:(vg * 2 + tq + 1) * 512],
                            WQap(sv + vg)[:, a * 128:(a + 1) * 128],
                            P1[:, a * T + th * 1024 + tq * 512:
                               a * T + th * 1024 + (tq + 1) * 512],
                            start=(a == 0), stop=last),
                            reads=WQreg(sv + vg) +
                                  R("P1", a * T + th * 1024 + tq * 512,
                                    a * T + th * 1024 + (tq + 1) * 512, es=2),
                            writes=[RR(vg * 2 + tq)], inc=last)
            hsl = th
            h2regs = R("CBF", 2048 + hsl * 1024, 2048 + (hsl + 1) * 1024, es=2)
            b.op("vector", lambda v, PS=PS, th=th: v.tensor_tensor(
                GS[:, 0:1024], PS[:, 1024:2048],
                dqa1rep[:, th * 1024:(th + 1) * 1024], OP.mult),
                reads=[RR(2), RR(3)] + R("P3", th * 1024, (th + 1) * 1024),
                writes=R("HBF2", 0, 2048, es=2))
            b.op("vector", lambda v, PS=PS, th=th: v.tensor_tensor(
                T2[:, 0:1024], PS[:, 0:1024],
                dqa1rep[:, th * 1024:(th + 1) * 1024], OP.mult),
                reads=[RR(0), RR(1)] + R("P3", th * 1024, (th + 1) * 1024),
                writes=R("CBF", 0, 2048, es=2))
            b.op("scalar", lambda s: s.activation(SIL[:, 0:1024], GS[:, 0:1024],
                                                  AF.Silu),
                 reads=R("HBF2", 0, 2048, es=2), writes=R("HBF2", 2048, 4096, es=2))
            b.op("vector", lambda v, hsl=hsl: v.tensor_tensor(
                H2T[:, hsl * 1024:(hsl + 1) * 1024], T2[:, 0:1024],
                SIL[:, 0:1024], OP.mult),
                reads=R("CBF", 0, 2048, es=2) + R("HBF2", 2048, 4096, es=2),
                writes=h2regs)
            if p == 0:
                b.op("vector", lambda v, th=th, hsl=hsl: v.tensor_copy(
                    MXB[:, th * 1024:(th + 1) * 1024],
                    H2T[:, hsl * 1024:(hsl + 1) * 1024]),
                    reads=h2regs, writes=[("MXA", th)])
                b.op("vector", lambda v, th=th, hsl=hsl: v.tensor_copy(
                    MNB[:, th * 1024:(th + 1) * 1024],
                    H2T[:, hsl * 1024:(hsl + 1) * 1024]),
                    reads=h2regs, writes=[("MXA", 2 + th)])
            else:
                b.op("vector", lambda v, th=th, hsl=hsl: v.tensor_tensor(
                    MXB[:, th * 1024:(th + 1) * 1024],
                    MXB[:, th * 1024:(th + 1) * 1024],
                    H2T[:, hsl * 1024:(hsl + 1) * 1024], OP.max),
                    reads=[("MXA", th)] + h2regs, writes=[("MXA", th)])
                b.op("vector", lambda v, th=th, hsl=hsl: v.tensor_tensor(
                    MNB[:, th * 1024:(th + 1) * 1024],
                    MNB[:, th * 1024:(th + 1) * 1024],
                    H2T[:, hsl * 1024:(hsl + 1) * 1024], OP.min),
                    reads=[("MXA", 2 + th)] + h2regs, writes=[("MXA", 2 + th)])
            b.dma("gpsimd", lambda g, p=p, th=th, hsl=hsl: g.dma_start(
                out=h2sp_t[:, p, th * 1024:(th + 1) * 1024],
                in_=H2T[:, hsl * 1024:(hsl + 1) * 1024]),
                reads=h2regs, writes=[("h2sp", p, th)])
        if p < 16:
            for ii in range(4):
                emit_abs_chunk(wout_chunk(4 * p + ii), NGV + 4 * p + ii)
        elif p == 16:
            b.op("vector", lambda v: v.reduce_sum(
                Z2S[0:128, 0:1], slots[:, NGV:NGV + NWO], axis=AX.X),
                reads=[("slots", NGV + i) for i in range(NWO)],
                writes=R("Z2S", 0, 512))
            b.op("tensor", lambda t: t.matmul(
                psB[0:1, 1536:1537], c_onesc[:], Z2S[0:128, 0:1],
                start=True, stop=True),
                reads=R("Z2S", 0, 512) + [("c_onesc", 0)], writes=[RB(3)])
            b.op("vector", lambda v: v.tensor_scalar(
                srow[0:1, 2:3], psB[0:1, 1536:1537], 1.0 / (D * float(OH)), None,
                OP.mult),
                reads=[RB(3)], writes=[("srow", 2)])
            b.op("vector", lambda v: v.tensor_scalar(
                srow[0:1, 2:3], srow[0:1, 2:3], 1e-5, None, OP.max),
                reads=[("srow", 2)], writes=[("srow", 2)])
            b.op("vector", lambda v: v.tensor_scalar(
                srow[0:1, 3:4], srow[0:1, 2:3], -0.5, None, OP.mult),
                reads=[("srow", 2)], writes=[("srow", 3)])
            b.op("tensor", lambda t: t.matmul(
                psB[:, 1536:1537], c_ones[:, 0:128], srow[0:1, 2:3],
                start=True, stop=True),
                reads=[("srow", 2), ("c_ones", 0)], writes=[RB(3)])
            b.op("vector", lambda v: v.tensor_scalar(
                colS2[:, 0:1], psB[:, 1536:1537], 0.5, None, OP.mult),
                reads=[RB(3)], writes=[("colS2", 0)])
            b.op("vector", lambda v: v.tensor_scalar(
                colS2[:, 1:2], psB[:, 1536:1537], -0.5, None, OP.mult),
                reads=[RB(3)], writes=[("colS2", 0)])
        elif woj[0] < PAIRS:
            for _ in range(2):
                if woj[0] < PAIRS:
                    emit_wout_prep(woj[0])
                    woj[0] += 1
    while woj[0] < PAIRS:
        emit_wout_prep(woj[0])
        woj[0] += 1

    RMXall = [("MXA", i) for i in range(4)]
    RMNall = [("MNA", i) for i in range(4)]
    b.op("vector", lambda v: v.tensor_scalar(
        MNB[:, 0:2048], MNB[:, 0:2048], -1.0, None, OP.mult),
        reads=[("MXA", 2), ("MXA", 3)], writes=[("MXA", 2), ("MXA", 3)])
    b.op("vector", lambda v: v.tensor_tensor(
        MXB[:, 0:2048], MXB[:, 0:2048], MNB[:, 0:2048], OP.max),
        reads=RMXall, writes=[("MXA", 0), ("MXA", 1)])
    for j in range(16):
        b.op("tensor", lambda t, j=j: t.transpose(
            PSB(1).bitcast(BF16)[:, 0:128], MXB[:, j * 128:(j + 1) * 128],
            c_idb[:]),
            reads=[("MXA", 0), ("MXA", 1), ("c_idb", 0)], writes=[RB(1)])
        b.op("vector", lambda v, j=j: v.reduce_max(
            MNA[:, j:j + 1], PSB(1).bitcast(BF16)[:, 0:128], axis=AX.X),
            reads=[RB(1)], writes=[("MNA", 0)])
    b.op("tensor", lambda t: t.transpose(
        PSB(1)[0:16, 0:128], MNA[:, 0:16], c_id[:]),
        reads=[("MNA", 0), ("c_id", 0)], writes=[RB(1)])
    b.op("vector", lambda v: v.tensor_copy(MNA[0:16, 64:192], PSB(1)[0:16, 0:128]),
         reads=[RB(1)], writes=[("MNA", 0)])
    b.dma("gpsimd", lambda g: g.dma_start(
        out=r_amax2.rearrange("e (j p) -> e j p", j=16),
        in_=MNA[0:16, 64:192]),
        reads=[("MNA", 0)], writes=R("P3", 0, 2048))
    b.op("vector", lambda v: v.tensor_scalar(
        r_amax2, r_amax2, 1e-5, None, OP.max),
        reads=R("P3", 0, 2048), writes=R("P3", 0, 2048))
    if debug:
        b.dma("sync", lambda s: s.dma_start(out=tapAM[:], in_=r_amax2),
              reads=R("P3", 0, 2048), q="Dout")

    S2B = [REP[:, 0:512], MXA[:, 0:512]]
    S2R = [[("REP", 0)], [("MXA", 0)]]
    DQB = [REP[:, 512:1024], MXA[:, 512:1024]]
    DQR = [[("REP", 1)], [("MXA", 1)]]
    M2B, M2R = MXA[:, 1024:1536], [("MXA", 2)]
    m2row = M2B[0:1, 0:512]

    def d_rowsQ(q):
        par = q % 2
        s2row = S2B[par][0:1, 0:512]
        dqrow = DQB[par][0:1, 0:512]
        b.op("vector", lambda v, q=q, s2row=s2row: v.reciprocal(
            s2row, r_amax2[0:1, q * 512:(q + 1) * 512]),
            reads=R("P3", 0, 2048), writes=S2R[par])
        b.op("vector", lambda v, s2row=s2row: v.tensor_scalar(
            s2row, s2row, QB, None, OP.mult),
            reads=S2R[par], writes=S2R[par])
        b.op("vector", lambda v, q=q, dqrow=dqrow: v.tensor_scalar(
            dqrow, r_amax2[0:1, q * 512:(q + 1) * 512], 1.0 / QB, None, OP.mult),
            reads=R("P3", 0, 2048), writes=DQR[par])
        b.op("vector", lambda v, dqrow=dqrow: v.tensor_scalar(
            dqrow, dqrow, srow[0:1, 3:4], None, OP.mult),
            reads=DQR[par] + [("srow", 3)], writes=DQR[par])
        bcast_row(s2row, S2R[par], PSB(2), RB(2), S2B[par], S2R[par])
        bcast_row(dqrow, DQR[par], PSB(3), RB(3), DQB[par], DQR[par])

    def d_chain(q, seg, j):
        par = q % 2
        jo = seg * 32 + j
        hq = jo % 8
        hsl = j % 2
        b.dma("sync", lambda s, q=q, jo=jo, hq=hq: s.dma_start(
            out=CBF[:, hq * 512:(hq + 1) * 512],
            in_=h2sp_t[:, jo, q * 512:(q + 1) * 512]),
            reads=[("h2sp", jo, q // 2)],
            writes=R("CBF", hq * 512, (hq + 1) * 512, es=2))
        src_ap, src_regs = RN[hsl][0], RN[hsl][1]
        b.op("vector", lambda v, hq=hq, src_ap=src_ap, par=par: v.tensor_tensor(
            src_ap, CBF[:, hq * 512:(hq + 1) * 512], S2B[par],
            OP.mult),
            reads=R("CBF", hq * 512, (hq + 1) * 512, es=2) +
                  S2R[par],
            writes=src_regs)
        round_neg(hsl,
                  P1[:, seg * 16384 + j * 512:seg * 16384 + (j + 1) * 512],
                  R("P1", seg * 16384 + j * 512,
                    seg * 16384 + (j + 1) * 512, es=2))

    def d_quant(q, seg):
        for j in range(32):
            d_chain(q, seg, j)

    def d_mm(q, seg, feeder=()):
        feeder = list(feeder)
        fi = 0
        par = q % 2
        late = []
        for dg in range(8):
            d0 = dg * 2
            rot = dg % 3
            PSm = [lambda k: PSA(k), lambda k: PSA(2 + k), lambda k: PSB(k)][rot]
            RRm = [lambda k: RA(k), lambda k: RA(2 + k), lambda k: RB(k)][rot]
            for blk in range(4):
                gj = seg * 32 + blk * 8
                sl = blk % 2
                b.dma("gpsimd", lambda g, dg=dg, gj=gj, sl=sl: g.dma_start(
                    out=WR[:, sl * 2048:(sl + 1) * 2048].rearrange(
                        "p (j c) -> p j c", j=8),
                    in_=wq2sp[dg, gj // 8]),
                    reads=[("wq2sp", j) for j in range(gj, gj + 8)],
                    writes=R("WR", sl * 2048, (sl + 1) * 2048, es=2),
                    q="Dwq")
                for jj in range(8):
                    jl = blk * 8 + jj
                    for ds in range(2):
                        b.op("tensor", lambda t, jl=jl, jj=jj, ds=ds, sl=sl,
                             seg=seg, PSm=PSm: t.matmul(
                            PSm(ds),
                            WR[:, sl * 2048 + jj * 256 + ds * 128:
                               sl * 2048 + jj * 256 + (ds + 1) * 128],
                            P1[:, seg * 16384 + jl * 512:
                               seg * 16384 + (jl + 1) * 512],
                            start=(jl == 0), stop=(jl == 31)),
                            reads=R("WR", sl * 2048, (sl + 1) * 2048, es=2) +
                                  R("P1", seg * 16384 + jl * 512,
                                    seg * 16384 + (jl + 1) * 512, es=2),
                            writes=[RRm(ds)] if jl in (0, 31) else [],
                            inc=((jj == 7 and ds == 1) or jl == 31))
            for f in late:
                f()
            late = []
            if seg == 0:
                for ds in range(2):
                    t_ = dg * 2 + ds
                    b.op("scalar", lambda s, ds=ds, t_=t_, PSm=PSm: s.copy(
                        P2[:, t_ * 512:(t_ + 1) * 512], PSm(ds)),
                        reads=[RRm(ds)], writes=R("P2", t_ * 512, (t_ + 1) * 512))
                take = -(-(len(feeder) - fi) // (8 - dg))
                for _ in range(take):
                    feeder[fi]()
                    fi += 1
                continue
            for ds in range(2):
                a = d0 + ds
                t_ = dg * 2 + ds
                b.dma("scalar", lambda s, q=q, a=a, ds=ds: s.dma_start(
                    out=X1S[:, ds * 512:(ds + 1) * 512],
                    in_=x1sp_t[:, a:a + 1, q * 512:(q + 1) * 512]),
                    reads=[("x1sp", a, q)],
                    writes=R("CBF2", ds * 1024, (ds + 1) * 1024, es=2),
                    q="Dscalar")
                b.op("vector", lambda v, ds=ds, t_=t_, PSm=PSm: v.tensor_tensor(
                    Z2S[:, ds * 512:(ds + 1) * 512], PSm(ds),
                    P2[:, t_ * 512:(t_ + 1) * 512], OP.add),
                    reads=[RRm(ds)] + R("P2", t_ * 512, (t_ + 1) * 512),
                    writes=R("Z2S", ds * 512, (ds + 1) * 512))
                b.op("vector", lambda v, ds=ds, par=par: v.tensor_tensor(
                    Z2S[:, ds * 512:(ds + 1) * 512],
                    Z2S[:, ds * 512:(ds + 1) * 512], DQB[par], OP.mult),
                    reads=R("Z2S", ds * 512, (ds + 1) * 512) + DQR[par],
                    writes=R("Z2S", ds * 512, (ds + 1) * 512))
                b.op("vector", lambda v, ds=ds: v.scalar_tensor_tensor(
                    Z2S[:, (2 + ds) * 512:(3 + ds) * 512],
                    X1S[:, ds * 512:(ds + 1) * 512], 2.0,
                    Z2S[:, ds * 512:(ds + 1) * 512], OP.mult, OP.add),
                    reads=R("CBF2", ds * 1024, (ds + 1) * 1024, es=2) +
                          R("Z2S", ds * 512, (ds + 1) * 512),
                    writes=R("Z2S", (2 + ds) * 512, (3 + ds) * 512))
                b.dma("scalar", lambda s, q=q, a=a, ds=ds: s.dma_start(
                    out=z2sp_t[:, a:a + 1, q * 512:(q + 1) * 512],
                    in_=Z2S.ap().rearrange("p (s t) -> p s t", s=4)[:, 2 + ds, :]),
                    reads=R("Z2S", (2 + ds) * 512, (3 + ds) * 512),
                    writes=[("z2sp", a, q)], q="Dscalar")
                def mk_sum(ds=ds, dg=dg):
                    b.op("tensor", lambda t, ds=ds, dg=dg: t.matmul(
                        psB[0:1, 1536:2048], c_onesc[:],
                        Z2S[:, (2 + ds) * 512:(3 + ds) * 512],
                        start=(dg == 0 and ds == 0), stop=(dg == 7 and ds == 1)),
                        reads=R("Z2S", (2 + ds) * 512, (3 + ds) * 512) +
                              [("c_onesc", 0)],
                        writes=[RB(3)] if (dg, ds) in ((0, 0), (7, 1)) else [],
                        inc=True)
                late.append(mk_sum)
                sqo = 2048 + ds * 512
                b.op("scalar", lambda s, ds=ds, sqo=sqo: s.activation(
                    P3[:, sqo:sqo + 512], Z2S[:, (2 + ds) * 512:(3 + ds) * 512],
                    AF.Square),
                    reads=R("Z2S", (2 + ds) * 512, (3 + ds) * 512),
                    writes=R("P3", sqo, sqo + 512))

                def mk_sumsq(ds=ds, dg=dg, sqo=sqo):
                    b.op("tensor", lambda t, ds=ds, dg=dg, sqo=sqo: t.matmul(
                        psB[0:1, 1024:1536], c_onesc[:], P3[:, sqo:sqo + 512],
                        start=(dg == 0 and ds == 0), stop=(dg == 7 and ds == 1)),
                        reads=R("P3", sqo, sqo + 512) + [("c_onesc", 0)],
                        writes=[RB(2)] if (dg, ds) in ((0, 0), (7, 1)) else [],
                        inc=True)
                late.append(mk_sumsq)
            take = -(-(len(feeder) - fi) // (8 - dg))
            for _ in range(take):
                feeder[fi]()
                fi += 1
        for f in late:
            f()

    def d_rowsL(q):
        b.op("vector", lambda v: v.tensor_scalar(
            m2row, psB[0:1, 1536:2048], 1.0 / D, None, OP.mult),
            reads=[RB(3)], writes=M2R)
        b.op("vector", lambda v: v.tensor_scalar(
            rs(1), psB[0:1, 1024:1536], 1.0 / D, None, OP.mult),
            reads=[RB(2)], writes=R("rows", 512, 1024))
        b.op("vector", lambda v: v.tensor_tensor(rs(0), m2row, m2row, OP.mult),
             reads=M2R, writes=R("rows", 0, 512))
        b.op("vector", lambda v: v.tensor_tensor(rs(1), rs(1), rs(0), OP.subtract),
             reads=R("rows", 0, 1024), writes=R("rows", 512, 1024))
        b.op("scalar", lambda s: s.activation(srep_row, rs(1), AF.Sqrt,
                                              bias=srow[0:1, 4:5]),
             reads=R("rows", 512, 1024) + [("srow", 0)], writes=[("MXA", 3)])
        b.op("vector", lambda v: v.reciprocal(srep_row, srep_row),
             reads=[("MXA", 3)], writes=[("MXA", 3)])
        bcast_row(m2row, M2R, PSB(2), RB(2), M2B, M2R)
        bcast_row(srep_row, [("MXA", 3)], PSB(3), RB(3),
                  SREP, [("MXA", 3)])

    def d_apply_a(q, a):
        sl = a % 2
        b.dma("scalar", lambda s, q=q, a=a, sl=sl: s.dma_start(
            out=Z2S.ap().rearrange("p (s t) -> p s t", s=4)[:, sl, :],
            in_=z2sp_t[:, a:a + 1, q * 512:(q + 1) * 512]),
            reads=[("z2sp", a, q)], writes=R("Z2S", sl * 512, (sl + 1) * 512),
            q="Dscalar")
        b.op("gpsimd", lambda g, sl=sl: g.tensor_tensor(
            Z2S[:, (2 + sl) * 512:(3 + sl) * 512],
            Z2S[:, sl * 512:(sl + 1) * 512], M2B, OP.subtract),
            reads=R("Z2S", sl * 512, (sl + 1) * 512) + M2R,
            writes=R("Z2S", (2 + sl) * 512, (3 + sl) * 512))
        b.op("gpsimd", lambda g, sl=sl: g.tensor_tensor(
            OTS[:, sl * 512:(sl + 1) * 512],
            Z2S[:, (2 + sl) * 512:(3 + sl) * 512], SREP, OP.mult),
            reads=R("Z2S", (2 + sl) * 512, (3 + sl) * 512) + [("MXA", 3)],
            writes=[("OTS", sl)])
        b.dma("gpsimd", lambda g, q=q, a=a, sl=sl: g.dma_start(
            out=out_t[:, a:a + 1, q * 512:(q + 1) * 512],
            in_=OTS.ap().rearrange("p (s t) -> p s t", s=2)[:, sl, :]),
            reads=[("OTS", sl)], writes=[], q="Dout")

    d_rowsQ(0)
    d_quant(0, 0)
    tail = []
    for q in range(NQ):
        f0 = [(lambda q=q, j=j: d_chain(q, 1, j)) for j in range(32)] + tail
        if q + 1 < NQ:
            f0.append(lambda q=q: d_rowsQ(q + 1))
        d_mm(q, 0, f0)
        if debug and q == 0:
            b.dma("sync", lambda s: s.dma_start(out=tapA2[:], in_=P1[:]),
                  reads=R("P1", 0, DT * T, es=2), q="Dout")
        if q + 1 < NQ:
            f1 = [(lambda q=q, j=j: d_chain(q + 1, 0, j)) for j in range(32)]
        else:
            f1 = []
        d_mm(q, 1, f1)
        tail = ([(lambda q=q: d_rowsL(q))] +
                [(lambda q=q, a=a: d_apply_a(q, a)) for a in range(DT)])
    for f in tail:
        f()

    for qq in ("Dout", "Dsync", "Dgpsimd", "Dscalar", "Dwq"):
        for i in range(8):
            b._wait("sync", f"{qq}.{i}", b.cnt[f"{qq}.{i}"])
    for e in ENGINES[1:]:
        b._wait("sync", "E" + e, b.cnt["E" + e])

    import contextlib
    with contextlib.ExitStack() as stack:
        semh = {name: stack.enter_context(nc.semaphore(name)) for name in b.semnames}
        with nc.Block() as block:
            def emit(engobj, ename):
                for item in b.stream[ename]:
                    if item[0] == "w":
                        engobj.wait_ge(semh[item[1]], item[2])
                    else:
                        instr = item[1](engobj)
                        if item[2] is not None:
                            instr.then_inc(semh[item[2]], item[3])

            @block.sync
            def _(sync):
                emit(sync, "sync")

            @block.gpsimd
            def _(gp):
                emit(gp, "gpsimd")

            @block.vector
            def _(v):
                emit(v, "vector")

            @block.scalar
            def _(sc):
                emit(sc, "scalar")

            @block.tensor
            def _(t):
                emit(t, "tensor")

    return nc


_CACHE = {}


def kernel(L0_attn_x, L0_attn_nu_log, L0_attn_theta_log, L0_attn_B_re,
           L0_attn_B_im, L0_attn_C_re, L0_attn_C_im, L0_attn_D,
           L0_ffn_norm_w, L0_ffn_w_gv, L0_ffn_w_out):
    x = np.asarray(L0_attn_x)
    BATCH = x.shape[0]
    import os
    dbg = bool(os.environ.get("KDEBUG"))
    key = ("nc", dbg)
    if key not in _CACHE:
        _CACHE[key] = build_program(debug=dbg)
    nc = _CACHE[key]

    wgvT = np.asarray(L0_ffn_w_gv).T
    wgvL = np.ascontiguousarray(
        wgvT.reshape(DT, 128, 128, 128).transpose(2, 1, 0, 3))
    woutT = np.ascontiguousarray(np.asarray(L0_ffn_w_out).T)
    BreT = np.ascontiguousarray(np.asarray(L0_attn_B_re).T)
    BimT = np.ascontiguousarray(np.asarray(L0_attn_B_im).T)
    CreT = np.ascontiguousarray(np.asarray(L0_attn_C_re).T)
    CimT = np.ascontiguousarray(np.asarray(L0_attn_C_im).T)
    nul = np.ascontiguousarray(np.asarray(L0_attn_nu_log).reshape(1, NS))
    thl = np.ascontiguousarray(np.asarray(L0_attn_theta_log).reshape(1, NS))
    Dv = np.ascontiguousarray(np.asarray(L0_attn_D).reshape(1, D))
    nw = np.ascontiguousarray(np.asarray(L0_ffn_norm_w).reshape(1, D))

    in_maps = []
    for bb in range(BATCH):
        in_maps.append({
            "xT": np.ascontiguousarray(x[bb].T),
            "BreT": BreT, "BimT": BimT, "CreT": CreT, "CimT": CimT,
            "nul": nul, "thl": thl, "Dv": Dv, "nw": nw,
            "wgvL": wgvL, "woutT": woutT,
        })
    res = run_bass_kernel_spmd(nc, in_maps, core_ids=list(range(BATCH)))
    out = np.empty((BATCH, T, D), dtype=np.float32)
    for bb in range(BATCH):
        out[bb] = np.asarray(res.results[bb]["outT"]).T
    return out
